# version 14 (final)
# baseline (speedup 1.0000x reference)
import math
import numpy as np
import ml_dtypes

import concourse.bass as bass
import concourse.mybir as mybir
from concourse.bass_utils import run_bass_kernel_spmd

F32 = mybir.dt.float32
BF16 = mybir.dt.bfloat16
I32 = mybir.dt.int32
AF = mybir.ActivationFunctionType
ALU = mybir.AluOpType
AXX = mybir.AxisListType.X

NCORES = 8
HD = 128
NLH = 4
TT = 512
PC = 128
SCL = 1.0 / math.sqrt(HD)
ENGS = ("te", "dv", "sc", "pl", "sy")


class Cfg:
    def __init__(self, B, SEQ, D, O):
        self.B, self.SEQ, self.D, self.O = B, SEQ, D, O
        self.TOK = B * SEQ
        self.NTB = SEQ // TT
        self.NT = B * self.NTB
        self.NDC = D // PC
        self.G = D // PC
        self.OWN = self.TOK // NCORES
        self.NTC = self.OWN // PC
        self.OT = min(TT, O)
        self.NOC = O // self.OT
        self.GSZ = min(4, self.NTC)
        self.NLT = self.TOK // PC
        self.NLO = self.OWN // PC
        self.NOB = NCORES * NLH


FULL = Cfg(B=2, SEQ=4096, D=4096, O=4096)


class P:

    def __init__(self):
        self.ops = {e: [] for e in ENGS}
        self.cnt = {}

    def n(self, sem):
        return self.cnt.get(sem, 0)

    def add(self, eng, fn, waits=None, inc=None, inc_k=1):
        w = {}
        for it in (waits or []):
            if it is None:
                continue
            s, v = it
            if v is None or v <= 0:
                continue
            w[s] = max(w.get(s, 0), v)
        self.ops[eng].append((w, fn, inc, inc_k))
        if inc is not None:
            self.cnt[inc] = self.cnt.get(inc, 0) + inc_k
            return self.cnt[inc]
        return None


def build(cfg: Cfg, phases="ABC", dbg=False):
    import os as _os
    c = cfg
    nc = bass.Bass("TRN2", target_bir_lowering=False,
                   detect_race_conditions=_os.environ.get("KRACE", "1") == "1")

    x_tm = nc.dram_tensor("x_tm", [c.TOK, c.D], F32, kind="ExternalInput").ap()
    x_own = nc.dram_tensor("x_own", [c.OWN, c.D], F32, kind="ExternalInput").ap()
    wq_c = nc.dram_tensor("wq_c", [c.D, NLH * HD], F32, kind="ExternalInput").ap()
    wk_c = nc.dram_tensor("wk_c", [c.D, HD], F32, kind="ExternalInput").ap()
    wv_c = nc.dram_tensor("wv_c", [c.D, HD], F32, kind="ExternalInput").ap()
    wo = nc.dram_tensor("wo", [NCORES * NLH * HD, c.O], F32,
                        kind="ExternalInput").ap()
    qT = nc.dram_tensor("qT", [c.G, PC, c.O], I32, kind="ExternalInput").ap()
    scalesT = nc.dram_tensor("scalesT", [c.G, c.O], F32, kind="ExternalInput").ap()
    zerosT = nc.dram_tensor("zerosT", [c.G, c.O], F32, kind="ExternalInput").ap()
    biasI = nc.dram_tensor("biasI", [1, c.O], F32, kind="ExternalInput").ap()
    coreofs = nc.dram_tensor("coreofs", [1, 1], I32, kind="ExternalInput").ap()
    out = nc.dram_tensor("out", [c.OWN, c.D], F32, kind="ExternalOutput").ap()
    if dbg:
        dbg_kt = nc.dram_tensor("dbg_kt", [PC, c.TOK], BF16,
                                kind="ExternalOutput").ap()
        dbg_vt = nc.dram_tensor("dbg_vt", [PC, c.TOK], BF16,
                                kind="ExternalOutput").ap()
        dbg_q = nc.dram_tensor("dbg_q", [NLH * PC, TT], BF16,
                               kind="ExternalOutput").ap()
        dbg_sm = nc.dram_tensor("dbg_sm", [1, 64], F32,
                                kind="ExternalOutput").ap()
        dbg_ag = nc.dram_tensor("dbg_ag", [NLH * HD, c.TOK], BF16,
                                kind="ExternalOutput").ap()
        dbg_xtm = nc.dram_tensor("dbg_xtm", [TT, c.D], BF16,
                                 kind="ExternalOutput").ap()
        dbg_xt = nc.dram_tensor("dbg_xt", [8 * PC, TT], BF16,
                                kind="ExternalOutput").ap()
        dbg_xn = nc.dram_tensor("dbg_xn", [c.OWN, c.D], BF16,
                                kind="ExternalOutput").ap()
        dbg_ago = nc.dram_tensor("dbg_ago", [NCORES * NLH * HD, c.OWN], BF16,
                                 kind="ExternalOutput").ap()

    xn_tm = nc.dram_tensor("xn_tm", [c.TOK, c.D], BF16, kind="Internal").ap()
    xn_own_tm = nc.dram_tensor("xn_own_tm", [c.OWN, c.D], BF16,
                               kind="Internal").ap()
    ag_in = nc.dram_tensor("ag_in", [NLH * HD, c.TOK], BF16, kind="Internal").ap()
    ag_out = nc.dram_tensor("ag_out", [NCORES * NLH * HD, c.TOK], BF16,
                            kind="Internal").ap()
    g_tm = nc.dram_tensor("g_tm", [c.OWN, c.O], BF16, kind="Internal").ap()

    ones_np = np.ones((PC, TT), dtype=ml_dtypes.bfloat16)
    ident_np = np.eye(PC, dtype=ml_dtypes.bfloat16)
    prot_np = np.zeros((PC, PC), dtype=np.float32)
    for j in range(PC // 2):
        prot_np[2 * j + 1, 2 * j] = -1.0
        prot_np[2 * j, 2 * j + 1] = 1.0
    esel_np = np.zeros((PC, c.G * 32), dtype=np.float32)
    for g in range(c.G):
        esel_np[:, 32 * g + (g % 32)] = 1.0
    gsel_np = np.zeros((c.G, c.G * PC), dtype=np.float32)
    for g in range(c.G):
        gsel_np[g, PC * g:PC * (g + 1)] = 1.0
    inv_freq = 10000.0 ** (-np.arange(0, HD, 2, dtype=np.float64) / HD)
    ang = np.arange(c.SEQ, dtype=np.float64)[None, :] * inv_freq[:, None]
    cos_np = np.repeat(np.cos(ang), 2, axis=0)
    sin_np = np.repeat(np.sin(ang), 2, axis=0)

    def bf(a):
        return np.ascontiguousarray(a.astype(ml_dtypes.bfloat16))

    ones_d = nc.inline_tensor(ones_np, "ones_c").ap()
    ident_d = nc.inline_tensor(ident_np, "ident_c").ap()
    prot_d = nc.inline_tensor(bf(prot_np), "prot_c").ap()
    esel_d = nc.inline_tensor(bf(esel_np), "esel_c").ap()
    gsel_d = nc.inline_tensor(bf(gsel_np), "gsel_c").ap()
    cos_d = nc.inline_tensor(bf(cos_np), "cos_c").ap()
    sin_d = nc.inline_tensor(bf(sin_np), "sin_c").ap()

    def sb(name, shape, dtype):
        return nc.alloc_sbuf_tensor(name, shape, dtype)

    ONES = sb("ONES", [PC, TT], BF16)
    IDENT = sb("IDENT", [PC, PC], BF16)
    PROT = sb("PROT", [PC, PC], BF16)
    ESEL = sb("ESEL", [PC, c.G * 32], BF16)
    GSEL = sb("GSEL", [c.G, c.G * PC], BF16)
    cofs_sb = sb("cofs_sb", [1, 1], I32)

    class TView:
        __slots__ = ("_ap",)

        def __init__(self, ap):
            self._ap = ap

        def ap(self):
            return self._ap

    A1BYTES = 163840
    ZA1 = nc.alloc_sbuf_tensor("ZA1", [PC, A1BYTES], mybir.dt.uint8)
    a1_16 = ZA1.bitcast(BF16).ap()
    a1_32 = ZA1.bitcast(F32).ap()
    A2BYTES = 36480
    ZA2 = nc.alloc_sbuf_tensor("ZA2", [PC, A2BYTES], mybir.dt.uint8)
    a2_16 = ZA2.bitcast(BF16).ap()
    a2_32 = ZA2.bitcast(F32).ap()

    _aptr = {1: [0], 2: [0]}

    def view(arena, dtype, shape, nbytes=None):
        p, n = shape
        its = 2 if dtype == BF16 else 4
        nb = n * its if nbytes is None else nbytes
        base = _aptr[arena][0]
        assert base % its == 0
        a = {(1, BF16): a1_16, (1, F32): a1_32,
             (2, BF16): a2_16, (2, F32): a2_32}[(arena, dtype)]
        v = a[0:p, base // its: base // its + n]
        _aptr[arena][0] = (base + nb + 31) // 32 * 32
        lim = A1BYTES if arena == 1 else A2BYTES
        assert _aptr[arena][0] <= lim, f"arena{arena} overflow"
        return TView(v)

    def rewind(arena):
        _aptr[arena][0] = 0

    lnx = [view(1, F32, [PC, c.D]) for i in range(2)]
    sq = view(1, F32, [PC, c.D])
    xnout = [view(1, BF16, [PC, c.D]) for i in range(2)]
    KT = view(1, BF16, [PC, c.TOK])
    VTM = view(1, BF16, [PC, c.TOK])
    wqs = view(1, BF16, [PC, c.NDC * NLH * HD])
    wks = view(1, BF16, [PC, c.NDC * HD])
    wvs = view(1, BF16, [PC, c.NDC * HD])
    cosb = view(1, BF16, [PC, c.SEQ])
    sinb = view(1, BF16, [PC, c.SEQ])
    XNTR = int(_os.environ.get("KXNTR", "8"))
    xnt = [view(2, BF16, [PC, TT]) for i in range(XNTR)]
    qrope = [view(2, BF16, [PC, TT]) for h in range(NLH)]
    qb_sb = view(2, BF16, [PC, TT])
    vT_sb = view(2, BF16, [PC, TT])
    t1_sb = view(2, F32, [PC, TT])
    t2_sb = view(2, F32, [PC, TT])
    pt = [view(2, BF16, [PC, TT]) for i in range(6)]
    oav = [view(2, BF16, [PC, TT]) for i in range(2)]
    rb_sb = view(2, F32, [PC, TT])
    sqn = [view(2, BF16, [PC, TT]) for i in range(2)]
    negmr = view(2, BF16, [1, NLH * PC])
    recf = view(2, F32, [1, TT])
    recb = view(2, BF16, [1, TT])
    stat = {}
    for nm in ("sum", "sumsq", "mean", "msq", "var", "std", "rstd", "negm",
               "t0"):
        for p2 in range(2):
            stat[(nm, p2)] = view(2, F32, [PC, 1])
    maxq4 = view(2, F32, [1, 8])
    runk2 = view(2, F32, [1, max(2, c.B)])
    m4 = view(2, F32, [1, NLH])
    negm4 = view(2, F32, [1, NLH])
    negm4b = view(2, BF16, [1, NLH])
    negmc = view(2, F32, [PC, NLH])

    rewind(1)
    xnO = view(1, BF16, [PC, c.NDC * c.OWN])
    Ofm = view(1, BF16, [PC, c.NOB * c.OWN])
    wT_sb = view(1, BF16, [PC, c.G * c.OT])
    rewind(2)
    QTR, WOR = 2, 2
    QB = min(4, c.G)
    WB = min(2, c.NOB)
    qt_sb = [view(2, BF16, [PC, QB * c.OT]) for i in range(QTR)]
    wob = [view(2, BF16, [PC, WB * c.OT]) for i in range(WOR)]
    xc_sb = [view(2, F32, [PC, c.OT]) for i in range(2)]
    out_sb = [view(2, F32, [PC, c.OT]) for i in range(2)]
    gsb = [view(2, BF16, [PC, c.OT]) for i in range(2)]
    grb = [view(2, BF16, [PC, c.OT]) for i in range(2)]
    smat = view(2, BF16, [32, c.NTC * PC])
    scb = [view(2, BF16, [c.G, c.OT]) for i in range(2)]
    zeb = [view(2, BF16, [c.G, c.OT]) for i in range(2)]
    bib = [view(2, BF16, [1, c.OT]) for i in range(2)]
    assert nc.sbuf_base <= nc.sbuf_top, \
        f"SBUF over: {nc.sbuf_base} > {nc.sbuf_top}"

    ps = [nc.alloc_psum_tensor(f"ps{i}", [PC, 512], F32) for i in range(7)]
    psbf = nc.alloc_psum_tensor("psbf", [PC, 1024], BF16)
    ps7f = psbf.bitcast(F32)
    ps8 = ps + [ps7f]

    semnames = (["LNX0", "LNX1", "XNW0", "XNW1", "PRE", "PREW", "AGW0", "AGW1", "CC",
                 "CHDV", "CHSC",
                 "OFM", "XNO", "XC0", "XC1", "OUT0", "OUT1",
                 "TE", "DV", "SC", "PL",
                 "SCB0", "SCB1", "ZEB0", "ZEB1", "BIB0", "BIB1",
                 "GW0", "GW1", "GR0", "GR1"]
                + [f"XNT{i}" for i in range(XNTR)]
                + [f"QT{i}" for i in range(QTR)]
                + [f"WO{i}" for i in range(WOR)])

    pl = P()
    ck = {}
    REG = {}
    CH = {"dv": 0, "sc": 0}
    sems = {}

    def chn(instr, k):
        CH[k] += 1
        instr.then_inc(sems["CHDV" if k == "dv" else "CHSC"], 1)
        return CH[k]

    def chw(e, k, v):
        e.wait_ge(sems["CHDV" if k == "dv" else "CHSC"], v)

    def _emit_all(nc, pl, sems, semnames):
        from contextlib import ExitStack
        with ExitStack() as ctx:
            for s in semnames:
                sems[s] = ctx.enter_context(nc.semaphore(s))
            block = ctx.enter_context(nc.Block())

            def replay(e, key):
                for (w, fn, inc, inc_k) in pl.ops[key]:
                    for s, v in sorted(w.items()):
                        e.wait_ge(sems[s], v)
                    ret = fn(e)
                    if inc is not None:
                        if ret is None:
                            ret = e.nop()
                        if inc == "CC":
                            ret.then_inc(sems[inc])
                        else:
                            ret.then_inc(sems[inc], inc_k)

            @block.tensor
            def _(e):
                replay(e, "te")

            @block.vector
            def _(e):
                replay(e, "dv")

            @block.scalar
            def _(e):
                replay(e, "sc")

            @block.gpsimd
            def _(e):
                replay(e, "pl")

            @block.sync
            def _(e):
                replay(e, "sy")

        return nc

    def dma(eng, o, i, sem, waits=None, transpose=False):
        def f(e, o=o, i=i, transpose=transpose):
            return e.dma_start(out=o, in_=i, transpose=transpose)
        return pl.add(eng, f, waits=waits or [], inc=sem, inc_k=16)

    for (dst, src) in ((ONES, ones_d), (IDENT, ident_d), (PROT, prot_d),
                       (ESEL, esel_d), (GSEL, gsel_d), (cosb, cos_d),
                       (sinb, sin_d), (cofs_sb, coreofs)):
        dma("sy", dst.ap()[:, :], src, "PRE")
    wq_r = wq_c.rearrange("(c p) h -> p c h", p=PC)
    wk_r = wk_c.rearrange("(c p) h -> p c h", p=PC)
    wv_r = wv_c.rearrange("(c p) h -> p c h", p=PC)
    wq_o = wqs.ap().rearrange("p (c h) -> p c h", c=c.NDC)
    wk_o = wks.ap().rearrange("p (c h) -> p c h", c=c.NDC)
    wv_o = wvs.ap().rearrange("p (c h) -> p c h", c=c.NDC)
    step = 8
    for c0 in range(0, c.NDC, step):
        n = min(step, c.NDC - c0)
        dma("pl", wq_o[:, c0:c0 + n, :], wq_r[:, c0:c0 + n, :], "PREW")
        dma("pl", wk_o[:, c0:c0 + n, :], wk_r[:, c0:c0 + n, :], "PREW")
        dma("pl", wv_o[:, c0:c0 + n, :], wv_r[:, c0:c0 + n, :], "PREW")
    PRE_TOTAL = pl.n("PRE")
    PREW_TOTAL = pl.n("PREW")

    invD = 1.0 / c.D
    NLALL = c.NLT + c.NLO
    lnlast = {0: None, 1: None}

    def ln_emit(tl):
        p2 = tl % 2
        pv = lnlast[p2]
        lnlast[p2] = tl
        own = tl >= c.NLT
        row0 = (tl - c.NLT) * PC if own else tl * PC
        src = x_own if own else x_tm
        dst = xn_own_tm if own else xn_tm
        lx = lnx[p2]
        lv = dma("sy", lx.ap()[:, :], src[row0:row0 + PC, :], f"LNX{p2}",
                 waits=[("DV", ck.get(("ln", pv)))])

        def a1(e, p2=p2, lx=lx):
            return e.activation(sq.ap()[:, :], lx.ap()[:, :], AF.Square,
                                accum_out=stat[("sumsq", p2)].ap()[:, :])
        sc1 = pl.add("sc", a1, waits=[(f"LNX{p2}", lv),
                                      ("DV", ck.get(("lnstat", pv)))],
                     inc="SC")

        def d1(e, p2=p2, lx=lx):
            v1 = chn(e.reduce_sum(stat[("sum", p2)].ap()[:, :], lx.ap()[:, :],
                                  axis=AXX), "dv")
            chw(e, "dv", v1)
            v2 = chn(e.tensor_scalar_mul(stat[("mean", p2)].ap()[:, :],
                                         stat[("sum", p2)].ap()[:, :], invD),
                     "dv")
            v3 = chn(e.tensor_scalar_mul(stat[("msq", p2)].ap()[:, :],
                                         stat[("sumsq", p2)].ap()[:, :], invD),
                     "dv")
            chw(e, "dv", v2)
            v4 = chn(e.tensor_mul(stat[("t0", p2)].ap()[:, :],
                                  stat[("mean", p2)].ap()[:, :],
                                  stat[("mean", p2)].ap()[:, :]), "dv")
            chw(e, "dv", v4)
            v5 = chn(e.tensor_sub(stat[("var", p2)].ap()[:, :],
                                  stat[("msq", p2)].ap()[:, :],
                                  stat[("t0", p2)].ap()[:, :]), "dv")
            chw(e, "dv", v5)
            return e.tensor_scalar_add(stat[("var", p2)].ap()[:, :],
                                       stat[("var", p2)].ap()[:, :], 1e-5)
        dv1 = pl.add("dv", d1, waits=[(f"LNX{p2}", lv), ("SC", sc1)], inc="DV")
        ck[("lnstat", tl)] = dv1

        def a2(e, p2=p2):
            v1 = chn(e.activation(stat[("std", p2)].ap()[:, :],
                                  stat[("var", p2)].ap()[:, :], AF.Ln), "sc")
            chw(e, "sc", v1)
            return e.activation(stat[("rstd", p2)].ap()[:, :],
                                stat[("std", p2)].ap()[:, :], AF.Exp,
                                scale=-0.5)
        sc2 = pl.add("sc", a2, waits=[("DV", dv1)], inc="SC")

        xnb = xnout[p2]

        def d2(e, p2=p2, xnb=xnb, lx=lx):
            v2 = chn(e.tensor_mul(stat[("t0", p2)].ap()[:, :],
                                  stat[("mean", p2)].ap()[:, :],
                                  stat[("rstd", p2)].ap()[:, :]), "dv")
            chw(e, "dv", v2)
            v3 = chn(e.tensor_scalar_mul(stat[("negm", p2)].ap()[:, :],
                                         stat[("t0", p2)].ap()[:, :], -1.0),
                     "dv")
            chw(e, "dv", v3)
            return e.tensor_scalar(xnb.ap()[:, :], lx.ap()[:, :],
                                   stat[("rstd", p2)].ap()[:, :],
                                   stat[("negm", p2)].ap()[:, :],
                                   ALU.mult, ALU.add)
        dv2 = pl.add("dv", d2,
                     waits=[("SC", sc2), (f"XNW{p2}", ck.get(("xnw", pv)))],
                     inc="DV")
        ck[("ln", tl)] = dv2
        sv = dma("sc", dst[row0:row0 + PC, :], xnb.ap()[:, :], f"XNW{p2}",
                 waits=[("DV", dv2)])
        ck[("xnw", tl)] = sv

    gchunk = [0]

    def emit_one_transpose(tj, dc):
        lnneed = 4 * tj + 3
        gi = gchunk[0]
        slot = gi % XNTR
        w = [("TE", ck.get(("xntuse", gi - XNTR))),
             (f"XNW{lnneed % 2}", ck[("xnw", lnneed)]),
             (f"XNW{(lnneed - 1) % 2}", ck[("xnw", lnneed - 1)])]
        assert gi - XNTR < 0 or ("xntuse", gi - XNTR) in ck, \
            f"missing xntuse backref gi={gi}"
        v = dma("sy" if slot % 2 == 0 else "sc", xnt[slot].ap()[:, :],
                xn_tm[tj * TT:(tj + 1) * TT, dc * PC:(dc + 1) * PC],
                f"XNT{slot}", waits=w, transpose=True)
        ck[("xnt", gi)] = v
        gchunk[0] += 1

    def emit_transposes(tj):
        for dc in range(min(XNTR, c.NDC)):
            emit_one_transpose(tj, dc)

    def emit_tile(b, j):
        tj = b * c.NTB + j
        prev = ([("DV", ck.get(("finD", tj - 1))),
                 ("SC", ck.get(("expL", tj - 1)))]
                if tj > 0 else [("PRE", PRE_TOTAL), ("PREW", PREW_TOTAL)])
        for dc in range(c.NDC):
            gi = tj * c.NDC + dc
            slot = gi % XNTR
            w = [(f"XNT{slot}", ck[("xnt", gi)])] + (prev if dc == 0 else [])

            def fmm(e, dc=dc, slot=slot):
                st, sp = dc == 0, dc == c.NDC - 1
                r = None
                for h in range(NLH):
                    r = e.matmul(
                        ps[h].ap()[:, :],
                        wqs.ap()[:, (dc * NLH + h) * HD:(dc * NLH + h + 1) * HD],
                        xnt[slot].ap()[:, :], start=st, stop=sp)
                r = e.matmul(ps[4].ap()[:, :],
                             wks.ap()[:, dc * HD:(dc + 1) * HD],
                             xnt[slot].ap()[:, :], start=st, stop=sp)
                r = e.matmul(ps[5].ap()[:, :],
                             wvs.ap()[:, dc * HD:(dc + 1) * HD],
                             xnt[slot].ap()[:, :], start=st, stop=sp)
                return r
            v = pl.add("te", fmm, waits=w, inc="TE")
            ck[("xntuse", gi)] = v
            if dc + XNTR < c.NDC:
                emit_one_transpose(tj, dc + XNTR)
        ck[("proj", tj)] = pl.n("TE")

        cosl = cosb.ap()[:, j * TT:(j + 1) * TT]
        sinl = sinb.ap()[:, j * TT:(j + 1) * TT]

        for h in range(NLH):
            def aev(e, h=h):
                return e.activation(qb_sb.ap()[:, :], ps[h].ap()[:, :], AF.Copy)
            sv = pl.add("sc", aev,
                        waits=[("TE", ck.get(("protmm", tj, h - 1),
                                             ck[("proj", tj)]))], inc="SC")

            def tmm(e):
                return e.matmul(ps[6].ap()[:, :], PROT.ap()[:, :],
                                qb_sb.ap()[:, :], start=True, stop=True)
            tv = pl.add("te", tmm,
                        waits=[("SC", sv), ("DV", ck.get(("rope", tj, h - 1)))],
                        inc="TE")
            ck[("protmm", tj, h)] = tv

            def drope(e, h=h, cosl=cosl, sinl=sinl):
                v1 = chn(e.tensor_mul(t1_sb.ap()[:, :], ps[6].ap()[:, :], sinl),
                         "dv")
                v2 = chn(e.tensor_mul(t2_sb.ap()[:, :], ps[h].ap()[:, :], cosl),
                         "dv")
                chw(e, "dv", v2)
                return e.tensor_add(qrope[h].ap()[:, :], t1_sb.ap()[:, :],
                                    t2_sb.ap()[:, :])
            wr_ = [("TE", tv)]
            if tj == 0 and h == 0:
                wr_.append(("PRE", PRE_TOTAL))
            dvv = pl.add("dv", drope, waits=wr_, inc="DV")
            ck[("rope", tj, h)] = dvv

        def aevk(e):
            return e.activation(qb_sb.ap()[:, :], ps[4].ap()[:, :], AF.Copy)
        svk = pl.add("sc", aevk,
                     waits=[("TE", ck[("protmm", tj, NLH - 1)]),
                            ("DV", ck[("rope", tj, NLH - 1)])], inc="SC")

        def tmmk(e):
            return e.matmul(ps[6].ap()[:, :], PROT.ap()[:, :], qb_sb.ap()[:, :],
                            start=True, stop=True)
        tvk = pl.add("te", tmmk,
                     waits=[("SC", svk), ("DV", ck[("rope", tj, NLH - 1)])],
                     inc="TE")

        def dropek(e, tj=tj, cosl=cosl, sinl=sinl):
            v1 = chn(e.tensor_mul(t1_sb.ap()[:, :], ps[6].ap()[:, :], sinl),
                     "dv")
            v2 = chn(e.tensor_mul(t2_sb.ap()[:, :], ps[4].ap()[:, :], cosl),
                     "dv")
            chw(e, "dv", v2)
            return e.tensor_add(KT.ap()[:, tj * TT:(tj + 1) * TT],
                                t1_sb.ap()[:, :], t2_sb.ap()[:, :])
        dvk = pl.add("dv", dropek, waits=[("TE", tvk)], inc="DV")
        ck[("ropeK", tj)] = dvk

        def aevv(e):
            return e.activation(vT_sb.ap()[:, :], ps[5].ap()[:, :], AF.Copy)
        svv = pl.add("sc", aevv, waits=[("TE", ck[("proj", tj)])], inc="SC")

        def tvt(e):
            r = None
            for u in range(TT // PC):
                r = e.transpose(psbf.ap()[:, u * PC:(u + 1) * PC],
                                vT_sb.ap()[:, u * PC:(u + 1) * PC],
                                IDENT.ap()[:, :])
            return r
        tvv = pl.add("te", tvt,
                     waits=[("SC", svv), ("DV", ck.get(("vtm", tj - 1)))],
                     inc="TE")

        def dvt(e, tj=tj):
            return e.tensor_copy(VTM.ap()[:, tj * TT:(tj + 1) * TT],
                                 psbf.ap()[:, 0:TT])
        dvv2 = pl.add("dv", dvt, waits=[("TE", tvv)], inc="DV")
        ck[("vtm", tj)] = dvv2

        for h in range(NLH):
            def asq(e, h=h):
                return e.tensor_mul(sqn[h % 2].ap()[:, :],
                                    qrope[h].ap()[:, :], qrope[h].ap()[:, :])
            sv = pl.add("pl", asq,
                        waits=[("DV", ck[("rope", tj, h)]),
                               ("TE", ck.get(("normmm", tj, h - 2)))], inc="PL")

            def nmm(e, h=h):
                return e.matmul(ps[h].ap()[0:1, :], ONES.ap()[:, 0:1],
                                sqn[h % 2].ap()[:, :], start=True, stop=True)
            tv = pl.add("te", nmm, waits=[("PL", sv)], inc="TE")
            ck[("normmm", tj, h)] = tv

        def asqk(e, tj=tj):
            return e.tensor_mul(sqn[0].ap()[:, :],
                                KT.ap()[:, tj * TT:(tj + 1) * TT],
                                KT.ap()[:, tj * TT:(tj + 1) * TT])
        svk2 = pl.add("pl", asqk,
                      waits=[("DV", dvk),
                             ("TE", ck.get(("normmm", tj, NLH - 2)))], inc="PL")

        def nmmk(e):
            return e.matmul(ps[6].ap()[0:1, :], ONES.ap()[:, 0:1],
                            sqn[0].ap()[:, :], start=True, stop=True)
        tvk2 = pl.add("te", nmmk, waits=[("PL", svk2)], inc="TE")

        def dnorm(e, b=b, j=j):
            vq = 0
            for h in range(NLH):
                vq = chn(e.reduce_max(maxq4.ap()[0:1, h:h + 1],
                                      ps[h].ap()[0:1, :], axis=AXX), "dv")
            vk = chn(e.reduce_max(maxq4.ap()[0:1, 4:5], ps[6].ap()[0:1, :],
                                  axis=AXX), "dv")
            chw(e, "dv", vk)
            if j == 0:
                vr = chn(e.tensor_copy(runk2.ap()[0:1, b:b + 1],
                                       maxq4.ap()[0:1, 4:5]), "dv")
            else:
                vr = chn(e.tensor_max(runk2.ap()[0:1, b:b + 1],
                                      runk2.ap()[0:1, b:b + 1],
                                      maxq4.ap()[0:1, 4:5]), "dv")
            chw(e, "dv", vr)
            return e.tensor_scalar_mul(m4.ap()[0:1, :], maxq4.ap()[0:1, 0:NLH],
                                       runk2.ap()[0:1, b:b + 1])
        dvn = pl.add("dv", dnorm, waits=[("TE", tvk2)], inc="DV")

        def asqrt(e):
            v1 = chn(e.activation(m4.ap()[0:1, :], m4.ap()[0:1, :], AF.Ln),
                     "sc")
            chw(e, "sc", v1)
            return e.activation(m4.ap()[0:1, :], m4.ap()[0:1, :], AF.Exp,
                                scale=0.5)
        svq = pl.add("sc", asqrt, waits=[("DV", dvn)], inc="SC")

        def dneg(e):
            return e.tensor_scalar_mul(negm4b.ap()[0:1, :], m4.ap()[0:1, :],
                                       -SCL)
        dvg = pl.add("dv", dneg, waits=[("SC", svq)], inc="DV")

        def tneg(e):
            return e.matmul(ps[6].ap()[:, 0:NLH], ONES.ap()[0:1, 0:PC],
                            negm4b.ap()[0:1, 0:NLH], start=True, stop=True)
        tvneg = pl.add("te", tneg, waits=[("DV", dvg)], inc="TE")

        def dnegc(e):
            return e.tensor_copy(negmc.ap()[:, :], ps[6].ap()[:, 0:NLH])
        dvm = pl.add("dv", dnegc, waits=[("TE", tvneg)], inc="DV")
        ck[("negm", tj)] = dvm

        nk = (j + 1) * (TT // PC)
        for hp in range(NLH // 2):
            emit_attn_pair(b, j, tj, hp, (2 * hp, 2 * hp + 1), nk, dvm)
        ck[("finD", tj)] = pl.n("DV")
        ck[("expL", tj)] = pl.n("SC")

    def emit_attn_pair(b, j, tj, hp, hs, nk, dvm):
        firstw = [("DV", dvm)]
        if hp > 0:
            firstw.append(("DV", ck[("pairfin", tj, hp - 1)]))
        sched = []
        for kc in range(nk):
            sched.append(("S", 0, kc))
            sched.append(("S", 1, kc))
            if kc >= 1:
                sched.append(("A", 0, kc - 1))
                sched.append(("A", 1, kc - 1))
        sched.append(("A", 0, nk - 1))
        sched.append(("A", 1, nk - 1))
        for it, m2, kc in sched:
            h = hs[m2]
            slot = m2 * 3 + kc % 3
            diag = kc >= j * (TT // PC)
            if it == "S":
                w = list(firstw) if kc == 0 else []
                if kc >= 1:
                    w.append(("SC", ck[("exp", tj, h, kc - 1)]))

                def fS(e, h=h, m2=m2, kc=kc, b=b):
                    bank = ps[4 + m2]
                    return e.matmul(bank.ap()[:, :],
                                    KT.ap()[:, b * c.SEQ + kc * PC:
                                            b * c.SEQ + (kc + 1) * PC],
                                    qrope[h].ap()[:, :], start=True, stop=True)
                tv = pl.add("te", fS, waits=w, inc="TE")

                def fE(e, m2=m2, slot=slot, h=h):
                    return e.activation(pt[slot].ap()[:, :],
                                        ps[4 + m2].ap()[:, :], AF.Exp,
                                        scale=SCL,
                                        bias=negmc.ap()[:, h:h + 1])
                ev = pl.add("sc", fE,
                            waits=[("TE", tv),
                                   ("TE", ck.get(("av", tj, h, kc - 3)))],
                            inc="SC")
                ck[("exp", tj, h, kc)] = ev
                if diag:
                    def fM(e, slot=slot, j=j, kc=kc):
                        return e.affine_select(
                            out=pt[slot].ap()[:, :], in_=pt[slot].ap()[:, :],
                            pattern=[[1, TT]], compare_op=ALU.is_ge, fill=0.0,
                            base=j * TT - kc * PC, channel_multiplier=-1)
                    mv = pl.add("pl", fM, waits=[("SC", ev)], inc="PL")
                    ck[("mask", tj, h, kc)] = mv
            else:
                w = [("SC", ck[("exp", tj, h, kc)])]
                if kc >= j * (TT // PC):
                    w.append(("PL", ck[("mask", tj, h, kc)]))

                def fA(e, h=h, m2=m2, kc=kc, b=b, slot=slot, nk=nk):
                    st, sp = kc == 0, kc == nk - 1
                    e.matmul(ps[2 + m2].ap()[0:1, :], ONES.ap()[:, 0:1],
                             pt[slot].ap()[:, :], start=st, stop=sp)
                    return e.matmul(
                        ps[0 + m2].ap()[:, :],
                        VTM.ap()[:, b * c.SEQ + kc * PC:b * c.SEQ + (kc + 1) * PC],
                        pt[slot].ap()[:, :], start=st, stop=sp)
                av = pl.add("te", fA, waits=w, inc="TE")
                ck[("av", tj, h, kc)] = av
        recbs = [recb.ap()[0:1, :], sqn[1].ap()[0:1, 0:TT]]
        recfs = [recf.ap()[0:1, :], t1_sb.ap()[0:1, :]]
        rvs = {}
        for m2 in range(2):
            def fR(e, m2=m2):
                v1 = chn(e.activation(recfs[m2], ps[2 + m2].ap()[0:1, :],
                                      AF.Ln), "sc")
                chw(e, "sc", v1)
                return e.activation(recbs[m2], recfs[m2], AF.Exp, scale=-1.0)
            rvs[m2] = pl.add("sc", fR,
                             waits=[("TE", ck[("av", tj, hs[m2], nk - 1)])],
                             inc="SC")
        bvs = {}
        for m2 in range(2):
            def fB(e, m2=m2):
                if m2 == 0:
                    return e.matmul(ps[6].ap()[:, :], ONES.ap()[0:1, 0:PC],
                                    recbs[0], start=True, stop=True)
                return e.matmul(ps[4].ap()[:, :], ONES.ap()[0:1, 0:PC],
                                recbs[1], start=True, stop=True)
            wb = [("SC", rvs[m2])]
            assert m2 == 0 or rvs[m2] > ck[("exp", tj, hs[0], nk - 1)]
            bvs[m2] = pl.add("te", fB, waits=wb, inc="TE")
            ck[("psrmm", tj, hs[m2])] = bvs[m2]
        for m2 in range(2):
            h = hs[m2]
            gh = tj * NLH + h
            ob = oav[gh % 2]

            def fF(e, m2=m2, ob=ob):
                srcp = ps[6].ap()[:, :] if m2 == 0 else ps[4].ap()[:, :]
                rb = rb_sb.ap()[:, :] if m2 == 0 else t2_sb.ap()[:, :]
                v1 = chn(e.tensor_copy(rb, srcp), "dv")
                chw(e, "dv", v1)
                return e.scalar_tensor_tensor(ob.ap()[:, :],
                                              ps[0 + m2].ap()[:, :], 1.0,
                                              rb, ALU.mult, ALU.mult)
            wf = [("TE", bvs[m2]), (f"AGW{gh % 2}", ck.get(("agw", gh - 2)))]
            fv = pl.add("dv", fF, waits=wf, inc="DV")
            ck[("fin", tj, h)] = fv
            gt0 = b * c.SEQ + j * TT
            av2 = dma("pl", ag_in[h * PC:(h + 1) * PC, gt0:gt0 + TT],
                      ob.ap()[:, :], f"AGW{gh % 2}", waits=[("DV", fv)])
            ck[("agw", gh)] = av2
        ck[("pairfin", tj, hp)] = pl.n("DV")

    if "B" in phases:
        for q in range(min(8, 4 * c.NT)):
            ln_emit(q)
        for b in range(c.B):
            for j in range(c.NTB):
                tj = b * c.NTB + j
                emit_transposes(tj)
                emit_tile(b, j)
                if tj + 2 < c.NT:
                    for q in range(4):
                        ln_emit(4 * (tj + 2) + q)
                if 2 <= tj < 2 + c.NLO:
                    ln_emit(c.NLT + tj - 2)
    else:
        for tl in range(NLALL):
            ln_emit(tl)
        ck["B_TE"] = pl.n("TE")
        ck["B_DV"] = pl.n("DV")
        ck["B_SC"] = pl.n("SC")

    ck["B_TE"] = pl.n("TE")
    ck["B_DV"] = pl.n("DV")
    ck["B_SC"] = pl.n("SC")
    ck["ln_all"] = max(v for k, v in ck.items()
                       if isinstance(k, tuple) and k[0] == "ln")
    AGW_TOT = {0: pl.n("AGW0"), 1: pl.n("AGW1")}
    XNW_TOT = {0: pl.n("XNW0"), 1: pl.n("XNW1")}

    if "C" not in phases:
        bw = [("XNW0", pl.n("XNW0")), ("XNW1", pl.n("XNW1")),
              ("TE", ck["B_TE"]), ("DV", ck["B_DV"]), ("SC", ck["B_SC"])]
        if "G" in phases and "B" in phases:
            def fcc0(e):
                return e.collective_compute(
                    "AllGather", ALU.bypass,
                    replica_groups=[list(range(NCORES))],
                    ins=[ag_in.opt()], outs=[ag_out.opt()])
            pl.add("pl", fcc0,
                   waits=[("AGW0", pl.n("AGW0")), ("AGW1", pl.n("AGW1"))],
                   inc="CC", inc_k=1)
            bw.append(("CC", 1))
        if dbg:
            wB = [("AGW0", pl.n("AGW0")), ("AGW1", pl.n("AGW1")),
                  ("TE", ck["B_TE"]), ("DV", ck["B_DV"])]
            dma("sy", dbg_kt, KT.ap()[:, :], "OUT0", waits=wB)
            dma("sy", dbg_vt, VTM.ap()[:, :], "OUT0", waits=wB)
            for h_ in range(NLH):
                dma("sy", dbg_q[h_ * PC:(h_ + 1) * PC, :],
                    qrope[h_].ap()[:, :], "OUT0", waits=wB)
            dma("sy", dbg_sm[0:1, 0:8], maxq4.ap()[0:1, :], "OUT0", waits=wB)
            dma("sy", dbg_sm[0:1, 8:8 + max(2, c.B)], runk2.ap()[0:1, :],
                "OUT0", waits=wB)
            dma("sy", dbg_sm[0:1, 16:16 + NLH], m4.ap()[0:1, :], "OUT0",
                waits=wB)
            dma("sy", dbg_ag, ag_in, "OUT0", waits=wB)
            dma("sy", dbg_xtm, xn_tm[c.TOK - TT:c.TOK, :], "OUT0", waits=wB)
            for s_ in range(XNTR):
                dma("sy", dbg_xt[s_ * PC:(s_ + 1) * PC, :],
                    xnt[s_].ap()[:, :], "OUT0", waits=wB)
            dma("sy", dbg_xn, xn_own_tm[:, :], "OUT0",
                waits=[("XNW0", pl.n("XNW0")), ("XNW1", pl.n("XNW1"))])
        dma("sy", out, x_own, "OUT0", waits=bw)
        def ffin0(e):
            return e.nop()
        pl.add("pl", ffin0, waits=[("OUT0", pl.n("OUT0"))], inc="PL")
        return _emit_all(nc, pl, sems, semnames)

    def fcc(e):
        return e.collective_compute("AllGather", ALU.bypass,
                                    replica_groups=[list(range(NCORES))],
                                    ins=[ag_in.opt()], outs=[ag_out.opt()])
    pl.add("pl", fcc, waits=[("AGW0", AGW_TOT[0]), ("AGW1", AGW_TOT[1]),
                             ("PRE", PRE_TOTAL), ("PREW", PREW_TOTAL)],
           inc="CC", inc_k=1)

    bend = [("TE", ck["B_TE"]), ("DV", ck["B_DV"]), ("SC", ck["B_SC"]),
            ("CC", 1)]

    w0 = [("DV", ck["ln_all"]), ("XNW0", XNW_TOT[0]), ("XNW1", XNW_TOT[1])]
    for ic in range(c.NDC):
        dma("sy", xnO.ap()[:, ic * c.OWN:(ic + 1) * c.OWN],
            xn_own_tm[0:c.OWN, ic * PC:(ic + 1) * PC], "XNO",
            waits=w0 if ic == 0 else [], transpose=True)
    XNO_TOT = pl.n("XNO")

    digs = [int(ch) for ch in phases if ch.isdigit()]
    stage = digs[0] if digs else 5
    only_wo = "W" in phases
    no_evac = "V" in phases
    noc_lim = 2 if "H" in phases else c.NOC

    for tc in range(c.NTC):
        def fsm(e, tc=tc):
            r = None
            for g in range(c.G):
                r = e.matmul(
                    ps[5].ap()[0:32, 0:PC], ESEL.ap()[:, 32 * g:32 * g + 32],
                    xnO.ap()[:, g * c.OWN + tc * PC:g * c.OWN + (tc + 1) * PC],
                    start=(g == 0), stop=(g == c.G - 1))
            return r
        w = [("XNO", XNO_TOT), ("SC", ck["B_SC"]), ("DV", ck["B_DV"]),
             ("AGW0", AGW_TOT[0]), ("AGW1", AGW_TOT[1]), ("TE", ck["B_TE"])]
        if tc > 0:
            w = [("DV", ck[("smev", tc - 1)])]
        tv = pl.add("te", fsm, waits=w, inc="TE")

        def fse(e, tc=tc):
            return e.tensor_copy(smat.ap()[:, tc * PC:(tc + 1) * PC],
                                 ps[5].ap()[0:32, 0:PC])
        dv = pl.add("dv", fse, waits=[("TE", tv)], inc="DV")
        ck[("smev", tc)] = dv
    SMAT_DV = pl.n("DV")

    NGR = (c.NTC + c.GSZ - 1) // c.GSZ
    qgi, wgi, xgi, ggi = [0], [0], [0], [0]

    def emit_dequant(ot, ic, par, sv1):
        qv, qslot, qsub = ck[("qtl", ot, ic)]

        def fbc(e, ic=ic, par=par):
            return e.matmul(ps[4 + ic % 2].ap()[:, :],
                            GSEL.ap()[:, PC * ic:PC * (ic + 1)],
                            scb[par].ap()[:, :], start=True, stop=True)
        wb = [(f"SCB{par}", sv1), (f"QT{qslot}", qv),
              ("DV", ck.get(("deq", ot, ic - 2)))]
        if ic < 2:
            wb += [("SC", ck["B_SC"]), ("DV", SMAT_DV)]
        bv = pl.add("te", fbc, waits=wb, inc="TE")

        def fdq(e, ic=ic, qslot=qslot, qsub=qsub):
            return e.tensor_mul(wT_sb.ap()[:, ic * c.OT:(ic + 1) * c.OT],
                                qt_sb[qslot].ap()[:, qsub * c.OT:
                                                  (qsub + 1) * c.OT],
                                ps[4 + ic % 2].ap()[:, :])
        wd = [("TE", bv), ("TE", ck.get(("wtuse", ot - 1, ic)))]
        if ot == 0 and ic < 2:
            wd += [("DV", ck["B_DV"])]
        dv = pl.add("dv", fdq, waits=wd, inc="DV")
        ck[("deq", ot, ic)] = dv
        if qsub == QB - 1 or ic == c.G - 1:
            ck[("deqg", (ot * c.G + ic) // QB)] = dv

    for ot in range(noc_lim):
        par = ot % 2
        sv1 = dma("pl", scb[par].ap()[:, :],
                  scalesT[:, ot * c.OT:(ot + 1) * c.OT], f"SCB{par}",
                  waits=[("TE", ck.get(("bcastL", ot - 2)))])
        sv2 = dma("pl", zeb[par].ap()[:, :],
                  zerosT[:, ot * c.OT:(ot + 1) * c.OT], f"ZEB{par}",
                  waits=[("TE", ck.get(("gpassL", ot - 2)))])
        sv3 = dma("pl", bib[par].ap()[:, :],
                  biasI[0:1, ot * c.OT:(ot + 1) * c.OT], f"BIB{par}",
                  waits=[("TE", ck.get(("gpassL", ot - 2)))])
        def emit_gpass_ic(ot, ic, tcs, tg, par, sv2, sv3):
            def fgp(e, ic=ic, tcs=tcs):
                st = ic == 0
                r = None
                for i2, tc in enumerate(tcs):
                    r = e.matmul(
                        ps[i2].ap()[:, :],
                        xnO.ap()[:, ic * c.OWN + tc * PC:
                                 ic * c.OWN + (tc + 1) * PC],
                        wT_sb.ap()[:, ic * c.OT:(ic + 1) * c.OT],
                        start=st, stop=False)
                return r
            w = [("DV", ck[("deq", ot, ic)])]
            if ic == 0:
                w += [(f"ZEB{par}", sv2), (f"BIB{par}", sv3),
                      ("DV", ck["B_DV"]),
                      ("SC", ck.get(("gsilL",)))]
            tv = pl.add("te", fgp, waits=w, inc="TE")
            if tg == NGR - 1:
                ck[("wtuse", ot, ic)] = tv

        if stage >= 3:
            tcs0 = list(range(0, min(c.GSZ, c.NTC)))
            for ic in range(c.G):
                if ic % QB == 0:
                    gq = qgi[0]
                    qslot = gq % QTR
                    wq2 = [("DV", ck.get(("deqg", gq - QTR)))]
                    if gq == 0:
                        wq2 += [("XNW0", XNW_TOT[0]), ("XNW1", XNW_TOT[1]),
                                ("TE", ck["B_TE"]),
                                ("AGW0", AGW_TOT[0]), ("AGW1", AGW_TOT[1])]
                    qt3o = qt_sb[qslot].ap().rearrange("p (g o) -> p g o", g=QB)
                    qt3i = qT.rearrange("g p o -> p g o")
                    qv = dma("pl", qt3o,
                             qt3i[:, ic:ic + QB, ot * c.OT:(ot + 1) * c.OT],
                             f"QT{qslot}", waits=wq2)
                    qgi[0] += 1
                ck[("qtl", ot, ic)] = (qv, qslot, ic % QB)
                emit_dequant(ot, ic, par, sv1)
                if stage >= 4 and ic >= 2:
                    emit_gpass_ic(ot, ic - 2, tcs0, 0, par, sv2, sv3)
            if stage >= 4:
                for ic in (c.G - 2, c.G - 1):
                    emit_gpass_ic(ot, ic, tcs0, 0, par, sv2, sv3)
        if stage < 4:
            ck[("bcastL", ot)] = pl.n("TE")
            ck[("gpassL", ot)] = pl.n("TE")
            continue
        for tg in range(NGR):
            tcs = list(range(tg * c.GSZ, min((tg + 1) * c.GSZ, c.NTC)))
            if tg > 0:
                for ic in range(c.G):
                    emit_gpass_ic(ot, ic, tcs, tg, par, sv2, sv3)

            def fgz(e, tcs=tcs, par=par):
                r = None
                for i2, tc in enumerate(tcs):
                    e.matmul(ps[i2].ap()[:, :], ONES.ap()[0:1, 0:PC],
                             bib[par].ap()[0:1, :], start=False, stop=False)
                    r = e.matmul(ps[i2].ap()[:, :],
                                 smat.ap()[0:c.G, tc * PC:(tc + 1) * PC],
                                 zeb[par].ap()[:, :], start=False, stop=True)
                return r
            tvz = pl.add("te", fgz, waits=[("DV", SMAT_DV)], inc="TE")

            for i2, tc in enumerate(tcs):
                gg = ggi[0]
                gslot = gg % 2

                def fsilg(e, i2=i2, gslot=gslot):
                    return e.activation(gsb[gslot].ap()[:, :],
                                        ps[i2].ap()[:, :], AF.Silu)
                sv = pl.add("sc", fsilg,
                            waits=[("TE", tvz),
                                   (f"GW{gslot}", ck.get(("gst", gg - 2)))],
                            inc="SC")
                gv = dma("sy", g_tm[tc * PC:(tc + 1) * PC,
                                    ot * c.OT:(ot + 1) * c.OT],
                         gsb[gslot].ap()[:, :], f"GW{gslot}",
                         waits=[("SC", sv)])
                ck[("gst", gg)] = gv
                ck[("gsil", gg)] = sv
                ggi[0] += 1
            ck[("gsilL",)] = pl.n("SC")
        ck[("bcastL", ot)] = pl.n("TE")
        ck[("gpassL", ot)] = pl.n("TE")
    GST_TOT = {0: pl.n("GW0"), 1: pl.n("GW1")}
    GSIL_END = pl.n("SC")

    def freg(e):
        r = e.alloc_register("ofsreg")
        e.reg_load(r, cofs_sb.ap()[0:1, 0:1])
        REG["off"] = e.snap(r, min_val=0, max_val=c.TOK - c.OWN)
        return None
    pl.add("pl", freg, waits=bend, inc=None)

    def fof(e):
        ago3 = ag_out.rearrange("(c p) t -> p c t", p=PC)
        ofm3 = Ofm.ap().rearrange("p (c t) -> p c t", c=c.NOB)
        return e.dma_start(out=ofm3, in_=ago3[:, :, bass.ds(REG["off"], c.OWN)])
    pl.add("pl", fof, waits=[], inc="OFM", inc_k=16)
    OFM_TOT = pl.n("OFM")

    for ot in range(noc_lim):
        if stage < 2:
            break
        for db in range(0, c.NOB, WB):
            gw = wgi[0]
            wslot = gw % WOR
            wo3o = wob[wslot].ap().rearrange("p (d o) -> p d o", d=WB)
            wo3i = wo.rearrange("(d p) o -> p d o", p=PC)
            wv_ = dma("pl", wo3o,
                      wo3i[:, db:db + WB, ot * c.OT:(ot + 1) * c.OT],
                      f"WO{wslot}",
                      waits=[("TE", ck.get(("wouse", gw - WOR)))])
            wgi[0] += 1
            if only_wo:
                continue

            def fsp(e, db=db, wslot=wslot):
                r = None
                for d2 in range(WB):
                    dc = db + d2
                    st, sp = dc == 0, dc == c.NOB - 1
                    for tc in range(c.NTC):
                        r = e.matmul(
                            ps8[tc].ap()[:, :],
                            Ofm.ap()[:, dc * c.OWN + tc * PC:
                                     dc * c.OWN + (tc + 1) * PC],
                            wob[wslot].ap()[:, d2 * c.OT:(d2 + 1) * c.OT],
                            start=st, stop=sp)
                return r
            w = [(f"WO{wslot}", wv_), ("OFM", OFM_TOT)]
            if db == 0:
                w += [("SC", GSIL_END), ("DV", ck.get(("epiL", 0))),
                      ("DV", SMAT_DV)]
            tv = pl.add("te", fsp, waits=w, inc="TE")
            ck[("wouse", gw)] = tv
        if only_wo or no_evac or stage < 5:
            continue
        SP_TE = pl.n("TE")
        for tc in range(c.NTC):
            gx = xgi[0]
            xslot = gx % 2
            oslot = gx % 2
            xv = dma("sy", xc_sb[xslot].ap()[:, :],
                     x_own[tc * PC:(tc + 1) * PC,
                           ot * c.OT:(ot + 1) * c.OT],
                     f"XC{xslot}", waits=[("DV", ck.get(("epi", gx - 2)))])
            glv = dma("sy", grb[xslot].ap()[:, :],
                      g_tm[tc * PC:(tc + 1) * PC,
                           ot * c.OT:(ot + 1) * c.OT],
                      f"GR{xslot}",
                      waits=([("DV", ck.get(("epi", gx - 2)))] +
                             ([("GW0", GST_TOT[0]),
                               ("GW1", GST_TOT[1])] if gx < 2 else [])))

            def fepi(e, tc=tc, xslot=xslot, oslot=oslot):
                v1 = chn(e.tensor_mul(out_sb[oslot].ap()[:, :],
                                      ps8[tc].ap()[:, :],
                                      grb[xslot].ap()[:, :]), "dv")
                chw(e, "dv", v1)
                return e.tensor_add(out_sb[oslot].ap()[:, :],
                                    out_sb[oslot].ap()[:, :],
                                    xc_sb[xslot].ap()[:, :])
            we = [(f"XC{xslot}", xv), (f"GR{xslot}", glv), ("TE", SP_TE),
                  (f"OUT{oslot}", ck.get(("outw", gx - 2)))]
            ev = pl.add("dv", fepi, waits=we, inc="DV")
            ck[("epi", gx)] = ev
            ck[("epiL", 0)] = ev
            ovv = dma("sc", out[tc * PC:(tc + 1) * PC,
                                ot * c.OT:(ot + 1) * c.OT],
                      out_sb[oslot].ap()[:, :], f"OUT{oslot}",
                      waits=[("DV", ev)])
            ck[("outw", gx)] = ovv
            xgi[0] += 1

    if stage < 5:
        bw2 = [("TE", pl.n("TE")), ("DV", pl.n("DV")), ("SC", pl.n("SC")),
               ("OFM", pl.n("OFM")), ("XNO", pl.n("XNO")), ("CC", 1)]
        dma("sy", out, x_own, "OUT0", waits=bw2)

    if dbg:
        wB = [("AGW0", AGW_TOT[0]), ("AGW1", AGW_TOT[1]),
              ("TE", ck["B_TE"]), ("DV", ck["B_DV"])]
        dma("sy", dbg_kt, KT.ap()[:, :], "OUT0", waits=wB)
        dma("sy", dbg_vt, VTM.ap()[:, :], "OUT0", waits=wB)
        for h_ in range(NLH):
            dma("sy", dbg_q[h_ * PC:(h_ + 1) * PC, :], qrope[h_].ap()[:, :],
                "OUT0", waits=wB)
        dma("sy", dbg_sm[0:1, 0:8], maxq4.ap()[0:1, :], "OUT0", waits=wB)
        dma("sy", dbg_sm[0:1, 8:8 + max(2, c.B)], runk2.ap()[0:1, :], "OUT0",
            waits=wB)
        dma("sy", dbg_sm[0:1, 16:16 + NLH], m4.ap()[0:1, :], "OUT0", waits=wB)
        dma("sy", dbg_ag, ag_in, "OUT0",
            waits=[("AGW0", AGW_TOT[0]), ("AGW1", AGW_TOT[1])])
        dma("sy", dbg_xn, xn_own_tm[:, :], "OUT0",
            waits=[("XNW0", XNW_TOT[0]), ("XNW1", XNW_TOT[1])])
        for dc_ in range(c.NOB):
            def fago(e, dc_=dc_):
                return e.dma_start(
                    out=dbg_ago[dc_ * PC:(dc_ + 1) * PC, :],
                    in_=Ofm.ap()[:, dc_ * c.OWN:(dc_ + 1) * c.OWN])
            pl.add("pl", fago, waits=[("OFM", pl.n("OFM"))], inc="OUT0",
                   inc_k=16)

    OUT_TOT = {0: pl.n("OUT0"), 1: pl.n("OUT1")}

    def ffin(e):
        return e.nop()
    pl.add("pl", ffin, waits=[("OUT0", OUT_TOT[0]), ("OUT1", OUT_TOT[1])],
           inc="PL")

    return _emit_all(nc, pl, sems, semnames)


def kernel(L0_attn_x, L0_attn_wq, L0_attn_wk, L0_attn_wv, L0_attn_wo,
           L0_ffn_qweight, L0_ffn_scales, L0_ffn_scaled_zeros, L0_ffn_bias):
    c = FULL
    x = np.ascontiguousarray(np.asarray(L0_attn_x, dtype=np.float32)
                             .reshape(c.TOK, c.D))
    wq = np.asarray(L0_attn_wq, dtype=np.float32)
    wk = np.asarray(L0_attn_wk, dtype=np.float32)
    wv = np.asarray(L0_attn_wv, dtype=np.float32)
    woa = np.ascontiguousarray(np.asarray(L0_attn_wo, dtype=np.float32))
    qTh = np.ascontiguousarray(np.asarray(L0_ffn_qweight, dtype=np.int32)
                               .transpose(1, 2, 0))
    scT = np.ascontiguousarray(
        np.asarray(L0_ffn_scales, dtype=np.float32)[:, :, 0].T)
    zeT = np.ascontiguousarray(
        np.asarray(L0_ffn_scaled_zeros, dtype=np.float32)[:, :, 0].T)
    bi = np.ascontiguousarray(np.asarray(L0_ffn_bias, dtype=np.float32)
                              .reshape(1, c.O))

    nc = build(c)
    in_maps = []
    for k in range(NCORES):
        in_maps.append({
            "x_tm": x,
            "x_own": np.ascontiguousarray(x[k * c.OWN:(k + 1) * c.OWN]),
            "wq_c": np.ascontiguousarray(wq[:, k * NLH * HD:(k + 1) * NLH * HD]),
            "wk_c": np.ascontiguousarray(wk[:, k * HD:(k + 1) * HD]),
            "wv_c": np.ascontiguousarray(wv[:, k * HD:(k + 1) * HD]),
            "wo": woa, "qT": qTh, "scalesT": scT, "zerosT": zeT, "biasI": bi,
            "coreofs": np.array([[k * c.OWN]], dtype=np.int32),
        })
    res = run_bass_kernel_spmd(nc, in_maps, core_ids=list(range(NCORES)))
    outp = np.concatenate([np.asarray(r["out"]) for r in res.results], axis=0)
    return outp.reshape(c.B, c.SEQ, c.D).astype(np.float32)


# version 15
# speedup vs baseline: 1.1177x; 1.1177x over previous
import math
import numpy as np
import ml_dtypes

import concourse.bass as bass
import concourse.mybir as mybir
from concourse.bass_utils import run_bass_kernel_spmd

F32 = mybir.dt.float32
BF16 = mybir.dt.bfloat16
I32 = mybir.dt.int32
AF = mybir.ActivationFunctionType
ALU = mybir.AluOpType
AXX = mybir.AxisListType.X

NCORES = 8
HD = 128
NLH = 4
TT = 512
PC = 128
SCL = 1.0 / math.sqrt(HD)
ENGS = ("te", "dv", "sc", "pl", "sy")


class Cfg:
    def __init__(self, B, SEQ, D, O):
        self.B, self.SEQ, self.D, self.O = B, SEQ, D, O
        self.TOK = B * SEQ
        self.NTB = SEQ // TT
        self.NT = B * self.NTB
        self.NDC = D // PC
        self.G = D // PC
        self.OWN = self.TOK // NCORES
        self.NTC = self.OWN // PC
        self.OT = min(TT, O)
        self.NOC = O // self.OT
        self.GSZ = min(4, self.NTC)
        self.NLT = self.TOK // PC
        self.NLO = self.OWN // PC
        self.NOB = NCORES * NLH


FULL = Cfg(B=2, SEQ=4096, D=4096, O=4096)


class P:

    def __init__(self):
        self.ops = {e: [] for e in ENGS}
        self.cnt = {}

    def n(self, sem):
        return self.cnt.get(sem, 0)

    def add(self, eng, fn, waits=None, inc=None, inc_k=1):
        w = {}
        for it in (waits or []):
            if it is None:
                continue
            s, v = it
            if v is None or v <= 0:
                continue
            w[s] = max(w.get(s, 0), v)
        self.ops[eng].append((w, fn, inc, inc_k))
        if inc is not None:
            self.cnt[inc] = self.cnt.get(inc, 0) + inc_k
            return self.cnt[inc]
        return None


def build(cfg: Cfg, phases="ABC", dbg=False):
    import os as _os
    c = cfg
    nc = bass.Bass("TRN2", target_bir_lowering=False,
                   detect_race_conditions=_os.environ.get("KRACE", "1") == "1")

    x_tm = nc.dram_tensor("x_tm", [c.TOK, c.D], F32, kind="ExternalInput").ap()
    x_own = nc.dram_tensor("x_own", [c.OWN, c.D], F32, kind="ExternalInput").ap()
    wq_c = nc.dram_tensor("wq_c", [c.D, NLH * HD], F32, kind="ExternalInput").ap()
    wk_c = nc.dram_tensor("wk_c", [c.D, HD], F32, kind="ExternalInput").ap()
    wv_c = nc.dram_tensor("wv_c", [c.D, HD], F32, kind="ExternalInput").ap()
    wo = nc.dram_tensor("wo", [NCORES * NLH * HD, c.O], F32,
                        kind="ExternalInput").ap()
    qT = nc.dram_tensor("qT", [c.G, PC, c.O], I32, kind="ExternalInput").ap()
    scalesT = nc.dram_tensor("scalesT", [c.G, c.O], F32, kind="ExternalInput").ap()
    zerosT = nc.dram_tensor("zerosT", [c.G, c.O], F32, kind="ExternalInput").ap()
    biasI = nc.dram_tensor("biasI", [1, c.O], F32, kind="ExternalInput").ap()
    coreofs = nc.dram_tensor("coreofs", [1, 1], I32, kind="ExternalInput").ap()
    out = nc.dram_tensor("out", [c.OWN, c.D], F32, kind="ExternalOutput").ap()
    if dbg:
        dbg_kt = nc.dram_tensor("dbg_kt", [PC, c.TOK], BF16,
                                kind="ExternalOutput").ap()
        dbg_vt = nc.dram_tensor("dbg_vt", [PC, c.TOK], BF16,
                                kind="ExternalOutput").ap()
        dbg_q = nc.dram_tensor("dbg_q", [NLH * PC, TT], BF16,
                               kind="ExternalOutput").ap()
        dbg_sm = nc.dram_tensor("dbg_sm", [1, 64], F32,
                                kind="ExternalOutput").ap()
        dbg_ag = nc.dram_tensor("dbg_ag", [NLH * HD, c.TOK], BF16,
                                kind="ExternalOutput").ap()
        dbg_xtm = nc.dram_tensor("dbg_xtm", [TT, c.D], BF16,
                                 kind="ExternalOutput").ap()
        dbg_xt = nc.dram_tensor("dbg_xt", [8 * PC, TT], BF16,
                                kind="ExternalOutput").ap()
        dbg_xn = nc.dram_tensor("dbg_xn", [c.OWN, c.D], BF16,
                                kind="ExternalOutput").ap()
        dbg_ago = nc.dram_tensor("dbg_ago", [NCORES * NLH * HD, c.OWN], BF16,
                                 kind="ExternalOutput").ap()

    xn_tm = nc.dram_tensor("xn_tm", [c.TOK, c.D], BF16, kind="Internal").ap()
    xn_own_tm = nc.dram_tensor("xn_own_tm", [c.OWN, c.D], BF16,
                               kind="Internal").ap()
    ag_in = nc.dram_tensor("ag_in", [NLH * HD, c.TOK], BF16, kind="Internal").ap()
    ag_out = nc.dram_tensor("ag_out", [NCORES * NLH * HD, c.TOK], BF16,
                            kind="Internal").ap()
    g_tm = nc.dram_tensor("g_tm", [c.OWN, c.O], BF16, kind="Internal").ap()

    ones_np = np.ones((PC, TT), dtype=ml_dtypes.bfloat16)
    ident_np = np.eye(PC, dtype=ml_dtypes.bfloat16)
    prot_np = np.zeros((PC, PC), dtype=np.float32)
    for j in range(PC // 2):
        prot_np[2 * j + 1, 2 * j] = -1.0
        prot_np[2 * j, 2 * j + 1] = 1.0
    esel_np = np.zeros((PC, c.G * 32), dtype=np.float32)
    for g in range(c.G):
        esel_np[:, 32 * g + (g % 32)] = 1.0
    gsel_np = np.zeros((c.G, c.G * PC), dtype=np.float32)
    for g in range(c.G):
        gsel_np[g, PC * g:PC * (g + 1)] = 1.0
    inv_freq = 10000.0 ** (-np.arange(0, HD, 2, dtype=np.float64) / HD)
    ang = np.arange(c.SEQ, dtype=np.float64)[None, :] * inv_freq[:, None]
    cos_np = np.repeat(np.cos(ang), 2, axis=0)
    sin_np = np.repeat(np.sin(ang), 2, axis=0)

    def bf(a):
        return np.ascontiguousarray(a.astype(ml_dtypes.bfloat16))

    ones_d = nc.inline_tensor(ones_np, "ones_c").ap()
    ident_d = nc.inline_tensor(ident_np, "ident_c").ap()
    prot_d = nc.inline_tensor(bf(prot_np), "prot_c").ap()
    esel_d = nc.inline_tensor(bf(esel_np), "esel_c").ap()
    gsel_d = nc.inline_tensor(bf(gsel_np), "gsel_c").ap()
    cos_d = nc.inline_tensor(bf(cos_np), "cos_c").ap()
    sin_d = nc.inline_tensor(bf(sin_np), "sin_c").ap()

    def sb(name, shape, dtype):
        return nc.alloc_sbuf_tensor(name, shape, dtype)

    ONES = sb("ONES", [PC, TT], BF16)
    IDENT = sb("IDENT", [PC, PC], BF16)
    PROT = sb("PROT", [PC, PC], BF16)
    ESEL = sb("ESEL", [PC, c.G * 32], BF16)
    GSEL = sb("GSEL", [c.G, c.G * PC], BF16)
    cofs_sb = sb("cofs_sb", [1, 1], I32)

    class TView:
        __slots__ = ("_ap",)

        def __init__(self, ap):
            self._ap = ap

        def ap(self):
            return self._ap

    A1BYTES = 163840
    ZA1 = nc.alloc_sbuf_tensor("ZA1", [PC, A1BYTES], mybir.dt.uint8)
    a1_16 = ZA1.bitcast(BF16).ap()
    a1_32 = ZA1.bitcast(F32).ap()
    A2BYTES = 36480
    ZA2 = nc.alloc_sbuf_tensor("ZA2", [PC, A2BYTES], mybir.dt.uint8)
    a2_16 = ZA2.bitcast(BF16).ap()
    a2_32 = ZA2.bitcast(F32).ap()

    _aptr = {1: [0], 2: [0]}

    def view(arena, dtype, shape, nbytes=None):
        p, n = shape
        its = 2 if dtype == BF16 else 4
        nb = n * its if nbytes is None else nbytes
        base = _aptr[arena][0]
        assert base % its == 0
        a = {(1, BF16): a1_16, (1, F32): a1_32,
             (2, BF16): a2_16, (2, F32): a2_32}[(arena, dtype)]
        v = a[0:p, base // its: base // its + n]
        _aptr[arena][0] = (base + nb + 31) // 32 * 32
        lim = A1BYTES if arena == 1 else A2BYTES
        assert _aptr[arena][0] <= lim, f"arena{arena} overflow"
        return TView(v)

    def rewind(arena):
        _aptr[arena][0] = 0

    lnx = [view(1, F32, [PC, c.D]) for i in range(2)]
    sq = view(1, F32, [PC, c.D])
    xnout = [view(1, BF16, [PC, c.D]) for i in range(2)]
    KT = view(1, BF16, [PC, c.TOK])
    VTM = view(1, BF16, [PC, c.TOK])
    wqs = view(1, BF16, [PC, c.NDC * NLH * HD])
    wks = view(1, BF16, [PC, c.NDC * HD])
    wvs = view(1, BF16, [PC, c.NDC * HD])
    cosb = view(1, BF16, [PC, c.SEQ])
    sinb = view(1, BF16, [PC, c.SEQ])
    XNTR = int(_os.environ.get("KXNTR", "8"))
    xnt = [view(2, BF16, [PC, TT]) for i in range(XNTR)]
    qrope = [view(2, BF16, [PC, TT]) for h in range(NLH)]
    qb_sb = view(2, BF16, [PC, TT])
    vT_sb = view(2, BF16, [PC, TT])
    t1_sb = view(2, F32, [PC, TT])
    t2_sb = view(2, F32, [PC, TT])
    pt = [view(2, BF16, [PC, TT]) for i in range(6)]
    oav = [view(2, BF16, [PC, TT]) for i in range(2)]
    rb_sb = view(2, F32, [PC, TT])
    sqn = [view(2, BF16, [PC, TT]) for i in range(2)]
    negmr = view(2, BF16, [1, NLH * PC])
    recf = view(2, F32, [1, TT])
    recb = view(2, BF16, [1, TT])
    stat = {}
    for nm in ("sum", "sumsq", "mean", "msq", "var", "std", "rstd", "negm",
               "t0"):
        for p2 in range(2):
            stat[(nm, p2)] = view(2, F32, [PC, 1])
    maxq4 = view(2, F32, [1, 8])
    runk2 = view(2, F32, [1, max(2, c.B)])
    m4 = view(2, F32, [1, NLH])
    negm4 = view(2, F32, [1, NLH])
    negm4b = view(2, BF16, [1, NLH])
    negmc = view(2, F32, [PC, NLH])

    rewind(1)
    xnO = view(1, BF16, [PC, c.NDC * c.OWN])
    Ofm = view(1, BF16, [PC, c.NOB * c.OWN])
    wT_sb = view(1, BF16, [PC, c.G * c.OT])
    rewind(2)
    QTR, WOR = 2, 2
    QB = min(4, c.G)
    WB = min(2, c.NOB)
    qt_sb = [view(2, BF16, [PC, QB * c.OT]) for i in range(QTR)]
    wob = [view(2, BF16, [PC, WB * c.OT]) for i in range(WOR)]
    xc_sb = [view(2, F32, [PC, c.OT]) for i in range(2)]
    out_sb = [view(2, F32, [PC, c.OT]) for i in range(2)]
    gsb = [view(2, BF16, [PC, c.OT]) for i in range(2)]
    grb = [view(2, BF16, [PC, c.OT]) for i in range(2)]
    smat = view(2, BF16, [32, c.NTC * PC])
    scb = [view(2, BF16, [c.G, c.OT]) for i in range(2)]
    zeb = [view(2, BF16, [c.G, c.OT]) for i in range(2)]
    bib = [view(2, BF16, [1, c.OT]) for i in range(2)]
    assert nc.sbuf_base <= nc.sbuf_top, \
        f"SBUF over: {nc.sbuf_base} > {nc.sbuf_top}"

    ps = [nc.alloc_psum_tensor(f"ps{i}", [PC, 512], F32) for i in range(7)]
    psbf = nc.alloc_psum_tensor("psbf", [PC, 1024], BF16)
    ps7f = psbf.bitcast(F32)
    ps8 = ps + [ps7f]

    semnames = (["LNX0", "LNX1", "XNW0", "XNW1", "PRE", "PREW", "AGW0", "AGW1", "CC",
                 "CHDV", "CHSC",
                 "OFM", "XNO", "XC0", "XC1", "OUT0", "OUT1",
                 "TE", "DV", "SC", "PL",
                 "SCB0", "SCB1", "ZEB0", "ZEB1", "BIB0", "BIB1",
                 "GW0", "GW1", "GR0", "GR1"]
                + [f"XNT{i}" for i in range(XNTR)]
                + [f"QT{i}" for i in range(QTR)]
                + [f"WO{i}" for i in range(WOR)])

    pl = P()
    ck = {}
    REG = {}
    CH = {"dv": 0, "sc": 0}
    sems = {}

    def chn(instr, k):
        CH[k] += 1
        instr.then_inc(sems["CHDV" if k == "dv" else "CHSC"], 1)
        return CH[k]

    def chw(e, k, v):
        e.wait_ge(sems["CHDV" if k == "dv" else "CHSC"], v)

    def _emit_all(nc, pl, sems, semnames):
        from contextlib import ExitStack
        with ExitStack() as ctx:
            for s in semnames:
                sems[s] = ctx.enter_context(nc.semaphore(s))
            block = ctx.enter_context(nc.Block())

            def replay(e, key):
                for (w, fn, inc, inc_k) in pl.ops[key]:
                    for s, v in sorted(w.items()):
                        e.wait_ge(sems[s], v)
                    ret = fn(e)
                    if inc is not None:
                        if ret is None:
                            ret = e.nop()
                        if inc == "CC":
                            ret.then_inc(sems[inc])
                        else:
                            ret.then_inc(sems[inc], inc_k)

            @block.tensor
            def _(e):
                replay(e, "te")

            @block.vector
            def _(e):
                replay(e, "dv")

            @block.scalar
            def _(e):
                replay(e, "sc")

            @block.gpsimd
            def _(e):
                replay(e, "pl")

            @block.sync
            def _(e):
                replay(e, "sy")

        return nc

    def dma(eng, o, i, sem, waits=None, transpose=False):
        def f(e, o=o, i=i, transpose=transpose):
            return e.dma_start(out=o, in_=i, transpose=transpose)
        return pl.add(eng, f, waits=waits or [], inc=sem, inc_k=16)

    for (dst, src) in ((ONES, ones_d), (IDENT, ident_d), (PROT, prot_d),
                       (ESEL, esel_d), (GSEL, gsel_d), (cosb, cos_d),
                       (sinb, sin_d), (cofs_sb, coreofs)):
        dma("sy", dst.ap()[:, :], src, "PRE")
    wq_r = wq_c.rearrange("(c p) h -> p c h", p=PC)
    wk_r = wk_c.rearrange("(c p) h -> p c h", p=PC)
    wv_r = wv_c.rearrange("(c p) h -> p c h", p=PC)
    wq_o = wqs.ap().rearrange("p (c h) -> p c h", c=c.NDC)
    wk_o = wks.ap().rearrange("p (c h) -> p c h", c=c.NDC)
    wv_o = wvs.ap().rearrange("p (c h) -> p c h", c=c.NDC)
    step = 8
    for c0 in range(0, c.NDC, step):
        n = min(step, c.NDC - c0)
        dma("pl", wq_o[:, c0:c0 + n, :], wq_r[:, c0:c0 + n, :], "PREW")
        dma("pl", wk_o[:, c0:c0 + n, :], wk_r[:, c0:c0 + n, :], "PREW")
        dma("pl", wv_o[:, c0:c0 + n, :], wv_r[:, c0:c0 + n, :], "PREW")
    PRE_TOTAL = pl.n("PRE")
    PREW_TOTAL = pl.n("PREW")

    invD = 1.0 / c.D
    NLALL = c.NLT + c.NLO
    lnlast = {0: None, 1: None}

    def ln_emit(tl):
        p2 = tl % 2
        pv = lnlast[p2]
        lnlast[p2] = tl
        own = tl >= c.NLT
        row0 = (tl - c.NLT) * PC if own else tl * PC
        src = x_own if own else x_tm
        dst = xn_own_tm if own else xn_tm
        lx = lnx[p2]
        lv = dma("sy", lx.ap()[:, :], src[row0:row0 + PC, :], f"LNX{p2}",
                 waits=[("DV", ck.get(("ln", pv)))])

        def a1(e, p2=p2, lx=lx):
            return e.activation(sq.ap()[:, :], lx.ap()[:, :], AF.Square,
                                accum_out=stat[("sumsq", p2)].ap()[:, :])
        sc1 = pl.add("sc", a1, waits=[(f"LNX{p2}", lv),
                                      ("DV", ck.get(("lnstat", pv)))],
                     inc="SC")

        def d1(e, p2=p2, lx=lx):
            v1 = chn(e.reduce_sum(stat[("sum", p2)].ap()[:, :], lx.ap()[:, :],
                                  axis=AXX), "dv")
            chw(e, "dv", v1)
            v2 = chn(e.tensor_scalar_mul(stat[("mean", p2)].ap()[:, :],
                                         stat[("sum", p2)].ap()[:, :], invD),
                     "dv")
            v3 = chn(e.tensor_scalar_mul(stat[("msq", p2)].ap()[:, :],
                                         stat[("sumsq", p2)].ap()[:, :], invD),
                     "dv")
            chw(e, "dv", v2)
            v4 = chn(e.tensor_mul(stat[("t0", p2)].ap()[:, :],
                                  stat[("mean", p2)].ap()[:, :],
                                  stat[("mean", p2)].ap()[:, :]), "dv")
            chw(e, "dv", v4)
            v5 = chn(e.tensor_sub(stat[("var", p2)].ap()[:, :],
                                  stat[("msq", p2)].ap()[:, :],
                                  stat[("t0", p2)].ap()[:, :]), "dv")
            chw(e, "dv", v5)
            return e.tensor_scalar_add(stat[("var", p2)].ap()[:, :],
                                       stat[("var", p2)].ap()[:, :], 1e-5)
        dv1 = pl.add("dv", d1, waits=[(f"LNX{p2}", lv), ("SC", sc1)], inc="DV")
        ck[("lnstat", tl)] = dv1

        def a2(e, p2=p2):
            v1 = chn(e.activation(stat[("std", p2)].ap()[:, :],
                                  stat[("var", p2)].ap()[:, :], AF.Ln), "sc")
            chw(e, "sc", v1)
            return e.activation(stat[("rstd", p2)].ap()[:, :],
                                stat[("std", p2)].ap()[:, :], AF.Exp,
                                scale=-0.5)
        sc2 = pl.add("sc", a2, waits=[("DV", dv1)], inc="SC")

        xnb = xnout[p2]

        def d2(e, p2=p2, xnb=xnb, lx=lx):
            v2 = chn(e.tensor_mul(stat[("t0", p2)].ap()[:, :],
                                  stat[("mean", p2)].ap()[:, :],
                                  stat[("rstd", p2)].ap()[:, :]), "dv")
            chw(e, "dv", v2)
            v3 = chn(e.tensor_scalar_mul(stat[("negm", p2)].ap()[:, :],
                                         stat[("t0", p2)].ap()[:, :], -1.0),
                     "dv")
            chw(e, "dv", v3)
            return e.tensor_scalar(xnb.ap()[:, :], lx.ap()[:, :],
                                   stat[("rstd", p2)].ap()[:, :],
                                   stat[("negm", p2)].ap()[:, :],
                                   ALU.mult, ALU.add)
        dv2 = pl.add("dv", d2,
                     waits=[("SC", sc2), (f"XNW{p2}", ck.get(("xnw", pv)))],
                     inc="DV")
        ck[("ln", tl)] = dv2
        sv = dma("sc", dst[row0:row0 + PC, :], xnb.ap()[:, :], f"XNW{p2}",
                 waits=[("DV", dv2)])
        ck[("xnw", tl)] = sv

    gchunk = [0]

    def emit_one_transpose(tj, dc):
        lnneed = 4 * tj + 3
        gi = gchunk[0]
        slot = gi % XNTR
        w = [("TE", ck.get(("xntuse", gi - XNTR))),
             (f"XNW{lnneed % 2}", ck[("xnw", lnneed)]),
             (f"XNW{(lnneed - 1) % 2}", ck[("xnw", lnneed - 1)])]
        assert gi - XNTR < 0 or ("xntuse", gi - XNTR) in ck, \
            f"missing xntuse backref gi={gi}"
        v = dma("sy" if slot % 2 == 0 else "sc", xnt[slot].ap()[:, :],
                xn_tm[tj * TT:(tj + 1) * TT, dc * PC:(dc + 1) * PC],
                f"XNT{slot}", waits=w, transpose=True)
        ck[("xnt", gi)] = v
        gchunk[0] += 1

    def emit_transposes(tj):
        for dc in range(min(XNTR, c.NDC)):
            emit_one_transpose(tj, dc)

    def emit_tile(b, j):
        tj = b * c.NTB + j
        prev = ([("DV", ck.get(("finD", tj - 1))),
                 ("SC", ck.get(("expL", tj - 1)))]
                if tj > 0 else [("PRE", PRE_TOTAL), ("PREW", PREW_TOTAL)])
        for dc in range(c.NDC):
            gi = tj * c.NDC + dc
            slot = gi % XNTR
            w = [(f"XNT{slot}", ck[("xnt", gi)])] + (prev if dc == 0 else [])

            def fmm(e, dc=dc, slot=slot):
                st, sp = dc == 0, dc == c.NDC - 1
                r = None
                for h in range(NLH):
                    r = e.matmul(
                        ps[h].ap()[:, :],
                        wqs.ap()[:, (dc * NLH + h) * HD:(dc * NLH + h + 1) * HD],
                        xnt[slot].ap()[:, :], start=st, stop=sp)
                r = e.matmul(ps[4].ap()[:, :],
                             wks.ap()[:, dc * HD:(dc + 1) * HD],
                             xnt[slot].ap()[:, :], start=st, stop=sp)
                r = e.matmul(ps[5].ap()[:, :],
                             wvs.ap()[:, dc * HD:(dc + 1) * HD],
                             xnt[slot].ap()[:, :], start=st, stop=sp)
                return r
            v = pl.add("te", fmm, waits=w, inc="TE")
            ck[("xntuse", gi)] = v
            if dc + XNTR < c.NDC:
                emit_one_transpose(tj, dc + XNTR)
        ck[("proj", tj)] = pl.n("TE")

        cosl = cosb.ap()[:, j * TT:(j + 1) * TT]
        sinl = sinb.ap()[:, j * TT:(j + 1) * TT]

        for h in range(NLH):
            def aev(e, h=h):
                return e.activation(qb_sb.ap()[:, :], ps[h].ap()[:, :], AF.Copy)
            sv = pl.add("sc", aev,
                        waits=[("TE", ck.get(("protmm", tj, h - 1),
                                             ck[("proj", tj)]))], inc="SC")

            def tmm(e):
                return e.matmul(ps[6].ap()[:, :], PROT.ap()[:, :],
                                qb_sb.ap()[:, :], start=True, stop=True)
            tv = pl.add("te", tmm,
                        waits=[("SC", sv), ("DV", ck.get(("rope", tj, h - 1)))],
                        inc="TE")
            ck[("protmm", tj, h)] = tv

            def drope(e, h=h, cosl=cosl, sinl=sinl):
                v1 = chn(e.tensor_mul(t1_sb.ap()[:, :], ps[6].ap()[:, :], sinl),
                         "dv")
                v2 = chn(e.tensor_mul(t2_sb.ap()[:, :], ps[h].ap()[:, :], cosl),
                         "dv")
                chw(e, "dv", v2)
                return e.tensor_add(qrope[h].ap()[:, :], t1_sb.ap()[:, :],
                                    t2_sb.ap()[:, :])
            wr_ = [("TE", tv)]
            if tj == 0 and h == 0:
                wr_.append(("PRE", PRE_TOTAL))
            dvv = pl.add("dv", drope, waits=wr_, inc="DV")
            ck[("rope", tj, h)] = dvv

        def aevk(e):
            return e.activation(qb_sb.ap()[:, :], ps[4].ap()[:, :], AF.Copy)
        svk = pl.add("sc", aevk,
                     waits=[("TE", ck[("protmm", tj, NLH - 1)]),
                            ("DV", ck[("rope", tj, NLH - 1)])], inc="SC")

        def tmmk(e):
            return e.matmul(ps[6].ap()[:, :], PROT.ap()[:, :], qb_sb.ap()[:, :],
                            start=True, stop=True)
        tvk = pl.add("te", tmmk,
                     waits=[("SC", svk), ("DV", ck[("rope", tj, NLH - 1)])],
                     inc="TE")

        def dropek(e, tj=tj, cosl=cosl, sinl=sinl):
            v1 = chn(e.tensor_mul(t1_sb.ap()[:, :], ps[6].ap()[:, :], sinl),
                     "dv")
            v2 = chn(e.tensor_mul(t2_sb.ap()[:, :], ps[4].ap()[:, :], cosl),
                     "dv")
            chw(e, "dv", v2)
            return e.tensor_add(KT.ap()[:, tj * TT:(tj + 1) * TT],
                                t1_sb.ap()[:, :], t2_sb.ap()[:, :])
        dvk = pl.add("dv", dropek, waits=[("TE", tvk)], inc="DV")
        ck[("ropeK", tj)] = dvk

        def aevv(e):
            return e.activation(vT_sb.ap()[:, :], ps[5].ap()[:, :], AF.Copy)
        svv = pl.add("sc", aevv, waits=[("TE", ck[("proj", tj)])], inc="SC")

        def tvt(e):
            r = None
            for u in range(TT // PC):
                r = e.transpose(psbf.ap()[:, u * PC:(u + 1) * PC],
                                vT_sb.ap()[:, u * PC:(u + 1) * PC],
                                IDENT.ap()[:, :])
            return r
        tvv = pl.add("te", tvt,
                     waits=[("SC", svv), ("DV", ck.get(("vtm", tj - 1)))],
                     inc="TE")

        def dvt(e, tj=tj):
            return e.tensor_copy(VTM.ap()[:, tj * TT:(tj + 1) * TT],
                                 psbf.ap()[:, 0:TT])
        dvv2 = pl.add("dv", dvt, waits=[("TE", tvv)], inc="DV")
        ck[("vtm", tj)] = dvv2

        for h in range(NLH):
            def asq(e, h=h):
                return e.tensor_mul(sqn[h % 2].ap()[:, :],
                                    qrope[h].ap()[:, :], qrope[h].ap()[:, :])
            sv = pl.add("pl", asq,
                        waits=[("DV", ck[("rope", tj, h)]),
                               ("TE", ck.get(("normmm", tj, h - 2)))], inc="PL")

            def nmm(e, h=h):
                return e.matmul(ps[h].ap()[0:1, :], ONES.ap()[:, 0:1],
                                sqn[h % 2].ap()[:, :], start=True, stop=True)
            tv = pl.add("te", nmm, waits=[("PL", sv)], inc="TE")
            ck[("normmm", tj, h)] = tv

        def asqk(e, tj=tj):
            return e.tensor_mul(sqn[0].ap()[:, :],
                                KT.ap()[:, tj * TT:(tj + 1) * TT],
                                KT.ap()[:, tj * TT:(tj + 1) * TT])
        svk2 = pl.add("pl", asqk,
                      waits=[("DV", dvk),
                             ("TE", ck.get(("normmm", tj, NLH - 2)))], inc="PL")

        def nmmk(e):
            return e.matmul(ps[6].ap()[0:1, :], ONES.ap()[:, 0:1],
                            sqn[0].ap()[:, :], start=True, stop=True)
        tvk2 = pl.add("te", nmmk, waits=[("PL", svk2)], inc="TE")

        def dnorm(e, b=b, j=j):
            vq = 0
            for h in range(NLH):
                vq = chn(e.reduce_max(maxq4.ap()[0:1, h:h + 1],
                                      ps[h].ap()[0:1, :], axis=AXX), "dv")
            vk = chn(e.reduce_max(maxq4.ap()[0:1, 4:5], ps[6].ap()[0:1, :],
                                  axis=AXX), "dv")
            chw(e, "dv", vk)
            if j == 0:
                vr = chn(e.tensor_copy(runk2.ap()[0:1, b:b + 1],
                                       maxq4.ap()[0:1, 4:5]), "dv")
            else:
                vr = chn(e.tensor_max(runk2.ap()[0:1, b:b + 1],
                                      runk2.ap()[0:1, b:b + 1],
                                      maxq4.ap()[0:1, 4:5]), "dv")
            chw(e, "dv", vr)
            return e.tensor_scalar_mul(m4.ap()[0:1, :], maxq4.ap()[0:1, 0:NLH],
                                       runk2.ap()[0:1, b:b + 1])
        dvn = pl.add("dv", dnorm, waits=[("TE", tvk2)], inc="DV")

        def asqrt(e):
            v1 = chn(e.activation(m4.ap()[0:1, :], m4.ap()[0:1, :], AF.Ln),
                     "sc")
            chw(e, "sc", v1)
            return e.activation(m4.ap()[0:1, :], m4.ap()[0:1, :], AF.Exp,
                                scale=0.5)
        svq = pl.add("sc", asqrt, waits=[("DV", dvn)], inc="SC")

        def dneg(e):
            return e.tensor_scalar_mul(negm4b.ap()[0:1, :], m4.ap()[0:1, :],
                                       -SCL)
        dvg = pl.add("dv", dneg, waits=[("SC", svq)], inc="DV")

        def tneg(e):
            return e.matmul(ps[6].ap()[:, 0:NLH], ONES.ap()[0:1, 0:PC],
                            negm4b.ap()[0:1, 0:NLH], start=True, stop=True)
        tvneg = pl.add("te", tneg, waits=[("DV", dvg)], inc="TE")

        def dnegc(e):
            return e.tensor_copy(negmc.ap()[:, :], ps[6].ap()[:, 0:NLH])
        dvm = pl.add("dv", dnegc, waits=[("TE", tvneg)], inc="DV")
        ck[("negm", tj)] = dvm

        nk = (j + 1) * (TT // PC)
        for hp in range(NLH // 2):
            emit_attn_pair(b, j, tj, hp, (2 * hp, 2 * hp + 1), nk, dvm)
        ck[("finD", tj)] = pl.n("DV")
        ck[("expL", tj)] = pl.n("SC")

    def emit_attn_pair(b, j, tj, hp, hs, nk, dvm):
        firstw = [("DV", dvm)]
        if hp > 0:
            firstw.append(("DV", ck[("pairfin", tj, hp - 1)]))
        sched = []
        for kc in range(nk):
            sched.append(("S", 0, kc))
            sched.append(("S", 1, kc))
            if kc >= 1:
                sched.append(("A", 0, kc - 1))
                sched.append(("A", 1, kc - 1))
        sched.append(("A", 0, nk - 1))
        sched.append(("A", 1, nk - 1))
        for it, m2, kc in sched:
            h = hs[m2]
            slot = m2 * 3 + kc % 3
            diag = kc >= j * (TT // PC)
            if it == "S":
                w = list(firstw) if kc == 0 else []
                if kc >= 1:
                    w.append(("SC", ck[("exp", tj, h, kc - 1)]))

                def fS(e, h=h, m2=m2, kc=kc, b=b):
                    bank = ps[4 + m2]
                    return e.matmul(bank.ap()[:, :],
                                    KT.ap()[:, b * c.SEQ + kc * PC:
                                            b * c.SEQ + (kc + 1) * PC],
                                    qrope[h].ap()[:, :], start=True, stop=True)
                tv = pl.add("te", fS, waits=w, inc="TE")

                def fE(e, m2=m2, slot=slot, h=h):
                    return e.activation(pt[slot].ap()[:, :],
                                        ps[4 + m2].ap()[:, :], AF.Exp,
                                        scale=SCL,
                                        bias=negmc.ap()[:, h:h + 1])
                ev = pl.add("sc", fE,
                            waits=[("TE", tv),
                                   ("TE", ck.get(("av", tj, h, kc - 3)))],
                            inc="SC")
                ck[("exp", tj, h, kc)] = ev
                if diag:
                    def fM(e, slot=slot, j=j, kc=kc):
                        return e.affine_select(
                            out=pt[slot].ap()[:, :], in_=pt[slot].ap()[:, :],
                            pattern=[[1, TT]], compare_op=ALU.is_ge, fill=0.0,
                            base=j * TT - kc * PC, channel_multiplier=-1)
                    mv = pl.add("pl", fM, waits=[("SC", ev)], inc="PL")
                    ck[("mask", tj, h, kc)] = mv
            else:
                w = [("SC", ck[("exp", tj, h, kc)])]
                if kc >= j * (TT // PC):
                    w.append(("PL", ck[("mask", tj, h, kc)]))

                def fA(e, h=h, m2=m2, kc=kc, b=b, slot=slot, nk=nk):
                    st, sp = kc == 0, kc == nk - 1
                    e.matmul(ps[2 + m2].ap()[0:1, :], ONES.ap()[:, 0:1],
                             pt[slot].ap()[:, :], start=st, stop=sp)
                    return e.matmul(
                        ps[0 + m2].ap()[:, :],
                        VTM.ap()[:, b * c.SEQ + kc * PC:b * c.SEQ + (kc + 1) * PC],
                        pt[slot].ap()[:, :], start=st, stop=sp)
                av = pl.add("te", fA, waits=w, inc="TE")
                ck[("av", tj, h, kc)] = av
        recbs = [recb.ap()[0:1, :], sqn[1].ap()[0:1, 0:TT]]
        recfs = [recf.ap()[0:1, :], t1_sb.ap()[0:1, :]]
        rvs = {}
        for m2 in range(2):
            def fR(e, m2=m2):
                v1 = chn(e.activation(recfs[m2], ps[2 + m2].ap()[0:1, :],
                                      AF.Ln), "sc")
                chw(e, "sc", v1)
                return e.activation(recbs[m2], recfs[m2], AF.Exp, scale=-1.0)
            rvs[m2] = pl.add("sc", fR,
                             waits=[("TE", ck[("av", tj, hs[m2], nk - 1)])],
                             inc="SC")
        bvs = {}
        for m2 in range(2):
            def fB(e, m2=m2):
                if m2 == 0:
                    return e.matmul(ps[6].ap()[:, :], ONES.ap()[0:1, 0:PC],
                                    recbs[0], start=True, stop=True)
                return e.matmul(ps[4].ap()[:, :], ONES.ap()[0:1, 0:PC],
                                recbs[1], start=True, stop=True)
            wb = [("SC", rvs[m2])]
            assert m2 == 0 or rvs[m2] > ck[("exp", tj, hs[0], nk - 1)]
            bvs[m2] = pl.add("te", fB, waits=wb, inc="TE")
            ck[("psrmm", tj, hs[m2])] = bvs[m2]
        for m2 in range(2):
            h = hs[m2]
            gh = tj * NLH + h
            ob = oav[gh % 2]

            def fC(e, m2=m2):
                srcp = ps[6].ap()[:, :] if m2 == 0 else ps[4].ap()[:, :]
                rb = rb_sb.ap()[:, :] if m2 == 0 else t2_sb.ap()[:, :]
                return e.activation(rb, srcp, AF.Copy)
            cv = pl.add("sc", fC, waits=[("TE", bvs[m2]),
                                         ("DV", ck.get(("finm", m2)))],
                        inc="SC")

            def fF(e, m2=m2, ob=ob):
                rb = rb_sb.ap()[:, :] if m2 == 0 else t2_sb.ap()[:, :]
                return e.scalar_tensor_tensor(ob.ap()[:, :],
                                              ps[0 + m2].ap()[:, :], 1.0,
                                              rb, ALU.mult, ALU.mult)
            wf = [("SC", cv), (f"AGW{gh % 2}", ck.get(("agw", gh - 2)))]
            fv = pl.add("dv", fF, waits=wf, inc="DV")
            ck[("fin", tj, h)] = fv
            ck[("finm", m2)] = fv
            gt0 = b * c.SEQ + j * TT
            av2 = dma("pl", ag_in[h * PC:(h + 1) * PC, gt0:gt0 + TT],
                      ob.ap()[:, :], f"AGW{gh % 2}", waits=[("DV", fv)])
            ck[("agw", gh)] = av2
        ck[("pairfin", tj, hp)] = pl.n("DV")

    if "B" in phases:
        for q in range(min(8, 4 * c.NT)):
            ln_emit(q)
        for b in range(c.B):
            for j in range(c.NTB):
                tj = b * c.NTB + j
                emit_transposes(tj)
                emit_tile(b, j)
                if tj + 2 < c.NT:
                    for q in range(4):
                        ln_emit(4 * (tj + 2) + q)
                if 2 <= tj < 2 + c.NLO:
                    ln_emit(c.NLT + tj - 2)
    else:
        for tl in range(NLALL):
            ln_emit(tl)
        ck["B_TE"] = pl.n("TE")
        ck["B_DV"] = pl.n("DV")
        ck["B_SC"] = pl.n("SC")

    ck["B_TE"] = pl.n("TE")
    ck["B_DV"] = pl.n("DV")
    ck["B_SC"] = pl.n("SC")
    ck["ln_all"] = max(v for k, v in ck.items()
                       if isinstance(k, tuple) and k[0] == "ln")
    AGW_TOT = {0: pl.n("AGW0"), 1: pl.n("AGW1")}
    XNW_TOT = {0: pl.n("XNW0"), 1: pl.n("XNW1")}

    if "C" not in phases:
        bw = [("XNW0", pl.n("XNW0")), ("XNW1", pl.n("XNW1")),
              ("TE", ck["B_TE"]), ("DV", ck["B_DV"]), ("SC", ck["B_SC"])]
        if "G" in phases and "B" in phases:
            def fcc0(e):
                return e.collective_compute(
                    "AllGather", ALU.bypass,
                    replica_groups=[list(range(NCORES))],
                    ins=[ag_in.opt()], outs=[ag_out.opt()])
            pl.add("pl", fcc0,
                   waits=[("AGW0", pl.n("AGW0")), ("AGW1", pl.n("AGW1"))],
                   inc="CC", inc_k=1)
            bw.append(("CC", 1))
        if dbg:
            wB = [("AGW0", pl.n("AGW0")), ("AGW1", pl.n("AGW1")),
                  ("TE", ck["B_TE"]), ("DV", ck["B_DV"])]
            dma("sy", dbg_kt, KT.ap()[:, :], "OUT0", waits=wB)
            dma("sy", dbg_vt, VTM.ap()[:, :], "OUT0", waits=wB)
            for h_ in range(NLH):
                dma("sy", dbg_q[h_ * PC:(h_ + 1) * PC, :],
                    qrope[h_].ap()[:, :], "OUT0", waits=wB)
            dma("sy", dbg_sm[0:1, 0:8], maxq4.ap()[0:1, :], "OUT0", waits=wB)
            dma("sy", dbg_sm[0:1, 8:8 + max(2, c.B)], runk2.ap()[0:1, :],
                "OUT0", waits=wB)
            dma("sy", dbg_sm[0:1, 16:16 + NLH], m4.ap()[0:1, :], "OUT0",
                waits=wB)
            dma("sy", dbg_ag, ag_in, "OUT0", waits=wB)
            dma("sy", dbg_xtm, xn_tm[c.TOK - TT:c.TOK, :], "OUT0", waits=wB)
            for s_ in range(XNTR):
                dma("sy", dbg_xt[s_ * PC:(s_ + 1) * PC, :],
                    xnt[s_].ap()[:, :], "OUT0", waits=wB)
            dma("sy", dbg_xn, xn_own_tm[:, :], "OUT0",
                waits=[("XNW0", pl.n("XNW0")), ("XNW1", pl.n("XNW1"))])
        dma("sy", out, x_own, "OUT0", waits=bw)
        def ffin0(e):
            return e.nop()
        pl.add("pl", ffin0, waits=[("OUT0", pl.n("OUT0"))], inc="PL")
        return _emit_all(nc, pl, sems, semnames)

    def fcc(e):
        return e.collective_compute("AllGather", ALU.bypass,
                                    replica_groups=[list(range(NCORES))],
                                    ins=[ag_in.opt()], outs=[ag_out.opt()])
    pl.add("pl", fcc, waits=[("AGW0", AGW_TOT[0]), ("AGW1", AGW_TOT[1]),
                             ("PRE", PRE_TOTAL), ("PREW", PREW_TOTAL)],
           inc="CC", inc_k=1)

    bend = [("TE", ck["B_TE"]), ("DV", ck["B_DV"]), ("SC", ck["B_SC"]),
            ("CC", 1)]

    w0 = [("DV", ck["ln_all"]), ("XNW0", XNW_TOT[0]), ("XNW1", XNW_TOT[1])]
    for ic in range(c.NDC):
        dma("sy", xnO.ap()[:, ic * c.OWN:(ic + 1) * c.OWN],
            xn_own_tm[0:c.OWN, ic * PC:(ic + 1) * PC], "XNO",
            waits=w0 if ic == 0 else [], transpose=True)
    XNO_TOT = pl.n("XNO")

    digs = [int(ch) for ch in phases if ch.isdigit()]
    stage = digs[0] if digs else 5
    only_wo = "W" in phases
    no_evac = "V" in phases
    noc_lim = 2 if "H" in phases else c.NOC

    for tc in range(c.NTC):
        def fsm(e, tc=tc):
            r = None
            for g in range(c.G):
                r = e.matmul(
                    ps[5].ap()[0:32, 0:PC], ESEL.ap()[:, 32 * g:32 * g + 32],
                    xnO.ap()[:, g * c.OWN + tc * PC:g * c.OWN + (tc + 1) * PC],
                    start=(g == 0), stop=(g == c.G - 1))
            return r
        w = [("XNO", XNO_TOT), ("SC", ck["B_SC"]), ("DV", ck["B_DV"]),
             ("AGW0", AGW_TOT[0]), ("AGW1", AGW_TOT[1]), ("TE", ck["B_TE"])]
        if tc > 0:
            w = [("DV", ck[("smev", tc - 1)])]
        tv = pl.add("te", fsm, waits=w, inc="TE")

        def fse(e, tc=tc):
            return e.tensor_copy(smat.ap()[:, tc * PC:(tc + 1) * PC],
                                 ps[5].ap()[0:32, 0:PC])
        dv = pl.add("dv", fse, waits=[("TE", tv)], inc="DV")
        ck[("smev", tc)] = dv
    SMAT_DV = pl.n("DV")

    NGR = (c.NTC + c.GSZ - 1) // c.GSZ
    qgi, wgi, xgi, ggi = [0], [0], [0], [0]

    def emit_dequant(ot, ic, par, sv1):
        qv, qslot, qsub = ck[("qtl", ot, ic)]

        def fbc(e, ic=ic, par=par):
            return e.matmul(ps[4 + ic % 2].ap()[:, :],
                            GSEL.ap()[:, PC * ic:PC * (ic + 1)],
                            scb[par].ap()[:, :], start=True, stop=True)
        wb = [(f"SCB{par}", sv1), (f"QT{qslot}", qv),
              ("DV", ck.get(("deq", ot, ic - 2)))]
        if ic < 2:
            wb += [("SC", ck["B_SC"]), ("DV", SMAT_DV)]
        bv = pl.add("te", fbc, waits=wb, inc="TE")

        def fdq(e, ic=ic, qslot=qslot, qsub=qsub):
            return e.tensor_mul(wT_sb.ap()[:, ic * c.OT:(ic + 1) * c.OT],
                                qt_sb[qslot].ap()[:, qsub * c.OT:
                                                  (qsub + 1) * c.OT],
                                ps[4 + ic % 2].ap()[:, :])
        wd = [("TE", bv), ("TE", ck.get(("wtuse", ot - 1, ic)))]
        if ot == 0 and ic < 2:
            wd += [("DV", ck["B_DV"])]
        dv = pl.add("dv", fdq, waits=wd, inc="DV")
        ck[("deq", ot, ic)] = dv
        if qsub == QB - 1 or ic == c.G - 1:
            ck[("deqg", (ot * c.G + ic) // QB)] = dv

    for ot in range(noc_lim):
        par = ot % 2
        sv1 = dma("pl", scb[par].ap()[:, :],
                  scalesT[:, ot * c.OT:(ot + 1) * c.OT], f"SCB{par}",
                  waits=[("TE", ck.get(("bcastL", ot - 2)))])
        sv2 = dma("pl", zeb[par].ap()[:, :],
                  zerosT[:, ot * c.OT:(ot + 1) * c.OT], f"ZEB{par}",
                  waits=[("TE", ck.get(("gpassL", ot - 2)))])
        sv3 = dma("pl", bib[par].ap()[:, :],
                  biasI[0:1, ot * c.OT:(ot + 1) * c.OT], f"BIB{par}",
                  waits=[("TE", ck.get(("gpassL", ot - 2)))])
        def emit_gpass_ic(ot, ic, tcs, tg, par, sv2, sv3):
            def fgp(e, ic=ic, tcs=tcs):
                st = ic == 0
                r = None
                for i2, tc in enumerate(tcs):
                    r = e.matmul(
                        ps[i2].ap()[:, :],
                        xnO.ap()[:, ic * c.OWN + tc * PC:
                                 ic * c.OWN + (tc + 1) * PC],
                        wT_sb.ap()[:, ic * c.OT:(ic + 1) * c.OT],
                        start=st, stop=False)
                return r
            w = [("DV", ck[("deq", ot, ic)])]
            if ic == 0:
                w += [(f"ZEB{par}", sv2), (f"BIB{par}", sv3),
                      ("DV", ck["B_DV"]),
                      ("SC", ck.get(("gsilL",)))]
            tv = pl.add("te", fgp, waits=w, inc="TE")
            if tg == NGR - 1:
                ck[("wtuse", ot, ic)] = tv

        if stage >= 3:
            tcs0 = list(range(0, min(c.GSZ, c.NTC)))
            for ic in range(c.G):
                if ic % QB == 0:
                    gq = qgi[0]
                    qslot = gq % QTR
                    wq2 = [("DV", ck.get(("deqg", gq - QTR)))]
                    if gq == 0:
                        wq2 += [("XNW0", XNW_TOT[0]), ("XNW1", XNW_TOT[1]),
                                ("TE", ck["B_TE"]),
                                ("AGW0", AGW_TOT[0]), ("AGW1", AGW_TOT[1])]
                    qt3o = qt_sb[qslot].ap().rearrange("p (g o) -> p g o", g=QB)
                    qt3i = qT.rearrange("g p o -> p g o")
                    qv = dma("pl", qt3o,
                             qt3i[:, ic:ic + QB, ot * c.OT:(ot + 1) * c.OT],
                             f"QT{qslot}", waits=wq2)
                    qgi[0] += 1
                ck[("qtl", ot, ic)] = (qv, qslot, ic % QB)
                emit_dequant(ot, ic, par, sv1)
                if stage >= 4 and ic >= 2:
                    emit_gpass_ic(ot, ic - 2, tcs0, 0, par, sv2, sv3)
            if stage >= 4:
                for ic in (c.G - 2, c.G - 1):
                    emit_gpass_ic(ot, ic, tcs0, 0, par, sv2, sv3)
        if stage < 4:
            ck[("bcastL", ot)] = pl.n("TE")
            ck[("gpassL", ot)] = pl.n("TE")
            continue
        for tg in range(NGR):
            tcs = list(range(tg * c.GSZ, min((tg + 1) * c.GSZ, c.NTC)))
            if tg > 0:
                for ic in range(c.G):
                    emit_gpass_ic(ot, ic, tcs, tg, par, sv2, sv3)

            def fgz(e, tcs=tcs, par=par):
                r = None
                for i2, tc in enumerate(tcs):
                    e.matmul(ps[i2].ap()[:, :], ONES.ap()[0:1, 0:PC],
                             bib[par].ap()[0:1, :], start=False, stop=False)
                    r = e.matmul(ps[i2].ap()[:, :],
                                 smat.ap()[0:c.G, tc * PC:(tc + 1) * PC],
                                 zeb[par].ap()[:, :], start=False, stop=True)
                return r
            tvz = pl.add("te", fgz, waits=[("DV", SMAT_DV)], inc="TE")

            for i2, tc in enumerate(tcs):
                gg = ggi[0]
                gslot = gg % 2

                def fsilg(e, i2=i2, gslot=gslot):
                    return e.activation(gsb[gslot].ap()[:, :],
                                        ps[i2].ap()[:, :], AF.Silu)
                sv = pl.add("sc", fsilg,
                            waits=[("TE", tvz),
                                   (f"GW{gslot}", ck.get(("gst", gg - 2)))],
                            inc="SC")
                gv = dma("sy", g_tm[tc * PC:(tc + 1) * PC,
                                    ot * c.OT:(ot + 1) * c.OT],
                         gsb[gslot].ap()[:, :], f"GW{gslot}",
                         waits=[("SC", sv)])
                ck[("gst", gg)] = gv
                ck[("gsil", gg)] = sv
                ggi[0] += 1
            ck[("gsilL",)] = pl.n("SC")
        ck[("bcastL", ot)] = pl.n("TE")
        ck[("gpassL", ot)] = pl.n("TE")
    GST_TOT = {0: pl.n("GW0"), 1: pl.n("GW1")}
    GSIL_END = pl.n("SC")

    def freg(e):
        r = e.alloc_register("ofsreg")
        e.reg_load(r, cofs_sb.ap()[0:1, 0:1])
        REG["off"] = e.snap(r, min_val=0, max_val=c.TOK - c.OWN)
        return None
    pl.add("pl", freg, waits=bend, inc=None)

    def fof(e):
        ago3 = ag_out.rearrange("(c p) t -> p c t", p=PC)
        ofm3 = Ofm.ap().rearrange("p (c t) -> p c t", c=c.NOB)
        return e.dma_start(out=ofm3, in_=ago3[:, :, bass.ds(REG["off"], c.OWN)])
    pl.add("pl", fof, waits=[], inc="OFM", inc_k=16)
    OFM_TOT = pl.n("OFM")

    for ot in range(noc_lim):
        if stage < 2:
            break
        for db in range(0, c.NOB, WB):
            gw = wgi[0]
            wslot = gw % WOR
            wo3o = wob[wslot].ap().rearrange("p (d o) -> p d o", d=WB)
            wo3i = wo.rearrange("(d p) o -> p d o", p=PC)
            wv_ = dma("pl", wo3o,
                      wo3i[:, db:db + WB, ot * c.OT:(ot + 1) * c.OT],
                      f"WO{wslot}",
                      waits=[("TE", ck.get(("wouse", gw - WOR)))])
            wgi[0] += 1
            if only_wo:
                continue

            def fsp(e, db=db, wslot=wslot):
                r = None
                for d2 in range(WB):
                    dc = db + d2
                    st, sp = dc == 0, dc == c.NOB - 1
                    for tc in range(c.NTC):
                        r = e.matmul(
                            ps8[tc].ap()[:, :],
                            Ofm.ap()[:, dc * c.OWN + tc * PC:
                                     dc * c.OWN + (tc + 1) * PC],
                            wob[wslot].ap()[:, d2 * c.OT:(d2 + 1) * c.OT],
                            start=st, stop=sp)
                return r
            w = [(f"WO{wslot}", wv_), ("OFM", OFM_TOT)]
            if db == 0:
                w += [("SC", GSIL_END), ("DV", ck.get(("epiL", 0))),
                      ("DV", SMAT_DV)]
            tv = pl.add("te", fsp, waits=w, inc="TE")
            ck[("wouse", gw)] = tv
        if only_wo or no_evac or stage < 5:
            continue
        SP_TE = pl.n("TE")
        for tc in range(c.NTC):
            gx = xgi[0]
            xslot = gx % 2
            oslot = gx % 2
            xv = dma("sy", xc_sb[xslot].ap()[:, :],
                     x_own[tc * PC:(tc + 1) * PC,
                           ot * c.OT:(ot + 1) * c.OT],
                     f"XC{xslot}", waits=[("DV", ck.get(("epi", gx - 2)))])
            glv = dma("sy", grb[xslot].ap()[:, :],
                      g_tm[tc * PC:(tc + 1) * PC,
                           ot * c.OT:(ot + 1) * c.OT],
                      f"GR{xslot}",
                      waits=([("DV", ck.get(("epi", gx - 2)))] +
                             ([("GW0", GST_TOT[0]),
                               ("GW1", GST_TOT[1])] if gx < 2 else [])))

            def fepi(e, tc=tc, xslot=xslot, oslot=oslot):
                v1 = chn(e.tensor_mul(out_sb[oslot].ap()[:, :],
                                      ps8[tc].ap()[:, :],
                                      grb[xslot].ap()[:, :]), "dv")
                chw(e, "dv", v1)
                return e.tensor_add(out_sb[oslot].ap()[:, :],
                                    out_sb[oslot].ap()[:, :],
                                    xc_sb[xslot].ap()[:, :])
            we = [(f"XC{xslot}", xv), (f"GR{xslot}", glv), ("TE", SP_TE),
                  (f"OUT{oslot}", ck.get(("outw", gx - 2)))]
            ev = pl.add("dv", fepi, waits=we, inc="DV")
            ck[("epi", gx)] = ev
            ck[("epiL", 0)] = ev
            ovv = dma("sc", out[tc * PC:(tc + 1) * PC,
                                ot * c.OT:(ot + 1) * c.OT],
                      out_sb[oslot].ap()[:, :], f"OUT{oslot}",
                      waits=[("DV", ev)])
            ck[("outw", gx)] = ovv
            xgi[0] += 1

    if stage < 5:
        bw2 = [("TE", pl.n("TE")), ("DV", pl.n("DV")), ("SC", pl.n("SC")),
               ("OFM", pl.n("OFM")), ("XNO", pl.n("XNO")), ("CC", 1)]
        dma("sy", out, x_own, "OUT0", waits=bw2)

    if dbg:
        wB = [("AGW0", AGW_TOT[0]), ("AGW1", AGW_TOT[1]),
              ("TE", ck["B_TE"]), ("DV", ck["B_DV"])]
        dma("sy", dbg_kt, KT.ap()[:, :], "OUT0", waits=wB)
        dma("sy", dbg_vt, VTM.ap()[:, :], "OUT0", waits=wB)
        for h_ in range(NLH):
            dma("sy", dbg_q[h_ * PC:(h_ + 1) * PC, :], qrope[h_].ap()[:, :],
                "OUT0", waits=wB)
        dma("sy", dbg_sm[0:1, 0:8], maxq4.ap()[0:1, :], "OUT0", waits=wB)
        dma("sy", dbg_sm[0:1, 8:8 + max(2, c.B)], runk2.ap()[0:1, :], "OUT0",
            waits=wB)
        dma("sy", dbg_sm[0:1, 16:16 + NLH], m4.ap()[0:1, :], "OUT0", waits=wB)
        dma("sy", dbg_ag, ag_in, "OUT0",
            waits=[("AGW0", AGW_TOT[0]), ("AGW1", AGW_TOT[1])])
        dma("sy", dbg_xn, xn_own_tm[:, :], "OUT0",
            waits=[("XNW0", XNW_TOT[0]), ("XNW1", XNW_TOT[1])])
        for dc_ in range(c.NOB):
            def fago(e, dc_=dc_):
                return e.dma_start(
                    out=dbg_ago[dc_ * PC:(dc_ + 1) * PC, :],
                    in_=Ofm.ap()[:, dc_ * c.OWN:(dc_ + 1) * c.OWN])
            pl.add("pl", fago, waits=[("OFM", pl.n("OFM"))], inc="OUT0",
                   inc_k=16)

    OUT_TOT = {0: pl.n("OUT0"), 1: pl.n("OUT1")}

    def ffin(e):
        return e.nop()
    pl.add("pl", ffin, waits=[("OUT0", OUT_TOT[0]), ("OUT1", OUT_TOT[1])],
           inc="PL")

    return _emit_all(nc, pl, sems, semnames)


def kernel(L0_attn_x, L0_attn_wq, L0_attn_wk, L0_attn_wv, L0_attn_wo,
           L0_ffn_qweight, L0_ffn_scales, L0_ffn_scaled_zeros, L0_ffn_bias):
    c = FULL
    x = np.ascontiguousarray(np.asarray(L0_attn_x, dtype=np.float32)
                             .reshape(c.TOK, c.D))
    wq = np.asarray(L0_attn_wq, dtype=np.float32)
    wk = np.asarray(L0_attn_wk, dtype=np.float32)
    wv = np.asarray(L0_attn_wv, dtype=np.float32)
    woa = np.ascontiguousarray(np.asarray(L0_attn_wo, dtype=np.float32))
    qTh = np.ascontiguousarray(np.asarray(L0_ffn_qweight, dtype=np.int32)
                               .transpose(1, 2, 0))
    scT = np.ascontiguousarray(
        np.asarray(L0_ffn_scales, dtype=np.float32)[:, :, 0].T)
    zeT = np.ascontiguousarray(
        np.asarray(L0_ffn_scaled_zeros, dtype=np.float32)[:, :, 0].T)
    bi = np.ascontiguousarray(np.asarray(L0_ffn_bias, dtype=np.float32)
                              .reshape(1, c.O))

    nc = build(c)
    in_maps = []
    for k in range(NCORES):
        in_maps.append({
            "x_tm": x,
            "x_own": np.ascontiguousarray(x[k * c.OWN:(k + 1) * c.OWN]),
            "wq_c": np.ascontiguousarray(wq[:, k * NLH * HD:(k + 1) * NLH * HD]),
            "wk_c": np.ascontiguousarray(wk[:, k * HD:(k + 1) * HD]),
            "wv_c": np.ascontiguousarray(wv[:, k * HD:(k + 1) * HD]),
            "wo": woa, "qT": qTh, "scalesT": scT, "zerosT": zeT, "biasI": bi,
            "coreofs": np.array([[k * c.OWN]], dtype=np.int32),
        })
    res = run_bass_kernel_spmd(nc, in_maps, core_ids=list(range(NCORES)))
    outp = np.concatenate([np.asarray(r["out"]) for r in res.results], axis=0)
    return outp.reshape(c.B, c.SEQ, c.D).astype(np.float32)
